# version 35
# baseline (speedup 1.0000x reference)
import numpy as np
from contextlib import ExitStack

import concourse.bass as bass
import concourse.mybir as mybir
from concourse.bass_utils import run_bass_kernel_spmd

F32 = mybir.dt.float32
BF16 = mybir.dt.bfloat16
AF = mybir.ActivationFunctionType
ALU = mybir.AluOpType
AX = mybir.AxisListType

NCORES = 8
T = 4096
D = 1024
KD = 8
H = 2048
NH = 16
NE = 10
NQ = 4
TQ = T // NQ
NA = 4
HD = 256

ENGINES = ("sync", "tensor", "vector", "scalar", "gpsimd")


class S:

    def __init__(self, stack, nc, name, is_dma=False, no_barrier=False):
        self.h = stack.enter_context(nc.semaphore(name))
        self.name = name
        self.is_dma = is_dma or no_barrier
        self.n = 0
        self.barrier = 0

    def mark(self):
        return (self, self.n)


_TRACE = None
DEBUG = False


class Plan:

    _block_ctr = [0]

    def __init__(self, nc, sems):
        self.nc = nc
        self.sems = sems
        self.ops = {e: [] for e in ENGINES}
        self.waited = {e: {} for e in ENGINES}
        self.block_id = Plan._block_ctr[0]
        Plan._block_ctr[0] += 1

    def op(self, eng, fn, waits=(), inc=None, incs=(), chain=None):
        all_incs = list(incs)
        if inc is not None:
            all_incs.append(inc)
        if chain is not None:
            waits = list(waits) + [(chain, chain.n)]
            all_incs.append((chain, 1))
        wl = []
        for s, v in waits:
            if v <= 0:
                continue
            if (not s.is_dma) and v <= s.barrier:
                continue
            prev = self.waited[eng].get(s.name, -1)
            if v <= prev:
                continue
            self.waited[eng][s.name] = v
            wl.append((s, v))
        self.ops[eng].append((fn, wl, all_incs))
        if _TRACE is not None:
            _TRACE.append((self.block_id, eng,
                           [(s.name, v) for s, v in wl],
                           [(s.name, k) for s, k in all_incs]))
        for s, k in all_incs:
            s.n += k

    dma = op

    def emit(self, no_gpsimd_drain=False):
        plan = self

        def mk(engname):
            def section(eng):
                for fn, wl, incs in plan.ops[engname]:
                    for s, v in wl:
                        eng.wait_ge(s.h, v)
                    inst = fn(eng)
                    if incs:
                        inst.then_inc(incs[0][0].h, incs[0][1])
                        for s, k in incs[1:]:
                            eng.nop().then_inc(s.h, k)
            return section

        with self.nc.Block(no_gpsimd_drain=no_gpsimd_drain) as blk:
            for engname in ENGINES:
                if self.ops[engname]:
                    getattr(blk, engname)(mk(engname))
        for s in self.sems.values():
            if not s.is_dma:
                s.barrier = s.n


def build_nc():
    nc = bass.Bass("TRN2", target_bir_lowering=False)

    xT_d = nc.dram_tensor("xT", [D, T], F32, kind="ExternalInput")
    w1T_d = nc.dram_tensor("w1T", [NE, D, H], F32, kind="ExternalInput")
    w3T_d = nc.dram_tensor("w3T", [NE, D, H], F32, kind="ExternalInput")
    w2T_d = nc.dram_tensor("w2T", [NE, H, D], F32, kind="ExternalInput")
    qwD_d = nc.dram_tensor("qwD", [128, 32, HD], F32, kind="ExternalInput")
    combD_d = nc.dram_tensor("combD", [128, 32, HD], F32, kind="ExternalInput")
    rwD_d = nc.dram_tensor("rwD", [128, KD, 8], F32, kind="ExternalInput")
    bias_d = nc.dram_tensor("bias", [1, 8], F32, kind="ExternalInput")
    ident_d = nc.inline_tensor(np.eye(128, dtype=np.float32), "ident")

    outT_d = nc.dram_tensor("outT", [D, T], F32, kind="ExternalOutput")

    taps = {}
    if DEBUG:
        for nm, shape in [("tap_xn1T", [128, KD, T]), ("tap_Kq", [128, KD, HD]),
                          ("tap_msum", [128, KD, HD]), ("tap_X2", [128, KD, T]),
                          ("tap_cw", [8, TQ]), ("tap_xn2T", [128, KD, TQ]),
                          ("tap_partial", [128, KD, TQ])]:
            taps[nm] = nc.dram_tensor(nm, shape, BF16, kind="ExternalOutput")

    w1s = nc.dram_tensor("w1s", [NE, D, H], BF16, kind="Internal")
    w3s = nc.dram_tensor("w3s", [NE, D, H], BF16, kind="Internal")
    w2s = nc.dram_tensor("w2s", [NE, H, D], BF16, kind="Internal")

    g = ExitStack()
    with g:
        sb = lambda name, shape, dt: g.enter_context(nc.sbuf_tensor(name, shape, dt))

        X = sb("X", [128, KD, T], BF16)
        idb = sb("idb", [128, 128], BF16)
        idf = sb("idf", [128, 128], F32)
        red_bf = sb("red_bf", [128, 1], BF16)
        ones_r_bf = sb("ones_r_bf", [1, 128], BF16)
        ones_r_f = sb("ones_r_f", [33, 128], F32)
        ONES_F = [ones_r_f[0:1, :], ones_r_f[32:33, :]]
        zw_bf = sb("zw_bf", [1, 512], BF16)
        cst = sb("cst", [128, 4], F32)
        c_eps = cst[:, 0:1]
        c_mhalf = cst[:, 1:2]
        c_mone = cst[:, 2:3]
        rw_bf = sb("rw_bf", [128, KD, 8], BF16)
        biasB = sb("biasB", [128, 8], F32)
        bias_row = sb("bias_row", [1, 8], F32)
        rows = sb("rows", [33, 5 * 256], F32)
        r_m = rows[0:1, 0:256]
        r_q = rows[0:1, 256:512]
        r_v = rows[0:1, 512:768]
        r_s = rows[0:1, 768:1024]
        r_ms = rows[0:1, 1024:1280]
        ROWS0 = (r_m, r_q, r_v, r_s, r_ms)
        ROWSB = tuple(rows[32:33, i * 256:(i + 1) * 256] for i in range(5))
        sq_c = sb("sq_c", [128, KD, 256], BF16)
        SQ0 = sq_c
        t_aff = sb("t_aff", [128, 256], F32)
        t_af2 = sb("t_af2", [128, 2, 256], F32)

        semstack = g.enter_context(ExitStack())
        sems = {}

        def mksem(name, is_dma=False, no_barrier=False):
            sems[name] = S(semstack, nc, name, is_dma, no_barrier)
            return sems[name]

        dX = mksem("dX", True)
        wsm = mksem("wsm", True)
        wl0 = mksem("wl0", True)
        wl1 = mksem("wl1", True)
        wl2 = mksem("wl2", True)
        os0 = mksem("os0", True)
        os1 = mksem("os1", True)
        wpe = [mksem(f"wpe{i}", True) for i in range(NE)]
        wfr = mksem("wfr")
        w2fr = mksem("w2fr")
        p1 = mksem("p1")
        p3 = mksem("p3")
        po = mksem("po")
        pm = mksem("pm")
        gf = mksem("gf")
        ve = mksem("ve")
        vch = mksem("vch")
        sch = mksem("sch")
        gch = mksem("gch", no_barrier=True)
        se = mksem("se")
        gg = mksem("gg", no_barrier=True)

        def tap(name, src_ap_fn):
            if not DEBUG:
                return
            h = taps[name]
            nd = len(list(h.shape))
            dst = h[(slice(None),) * nd]
            Pt = Plan(nc, sems)
            Pt.dma("sync", lambda e: e.dma_start(dst, src_ap_fn()),
                   inc=(wsm, 16))
            w = wsm.mark()
            Pt.op("sync", lambda e: e.nop(), waits=[w])
            Pt.emit(no_gpsimd_drain=True)

        def ln_stats(P, src_fn, stM, stQ, sB, msB, prev, extra_sq_waits=(),
                     sq_eng="gpsimd", rowset=None, sqbuf=None):
            mk = {}
            r_m, r_q, r_v, r_s, r_ms = rowset if rowset else ROWS0
            sq_c = sqbuf if sqbuf is not None else SQ0
            sq_sem = gg if sq_eng == "gpsimd" else se
            for k in range(KD):
                w = list(extra_sq_waits)
                if prev.get("pm_st"):
                    w.append(prev["pm_st"])
                if sq_eng == "gpsimd":
                    P.op("gpsimd",
                         lambda e, k=k: e.tensor_mul(sq_c[:, k, :], src_fn(k), src_fn(k)),
                         waits=w if k == 0 else [],
                         inc=(sq_sem, 1) if k == KD - 1 else None)
                else:
                    P.op("scalar",
                         lambda e, k=k: e.activation(sq_c[:, k, :], src_fn(k), AF.Square),
                         waits=w if k == 0 else [],
                         inc=(sq_sem, 1) if k == KD - 1 else None)
            mk["gg_sq"] = sq_sem.mark()
            for k in range(KD):
                w = list(extra_sq_waits)
                if prev.get("ve_rows"):
                    w.append(prev["ve_rows"])
                P.op("tensor",
                     lambda e, k=k: e.matmul(stM, red_bf[:, :], src_fn(k),
                                             start=(k == 0), stop=(k == KD - 1)),
                     waits=w if k == 0 else [])
            for k in range(KD):
                P.op("tensor",
                     lambda e, k=k: e.matmul(stQ, red_bf[:, :], sq_c[:, k, :],
                                             start=(k == 0), stop=(k == KD - 1)),
                     waits=[mk["gg_sq"]] if k == 0 else [],
                     inc=(pm, 1) if k == KD - 1 else None)
            mk["pm_st"] = pm.mark()
            P.op("vector", lambda e: e.tensor_copy(r_m[:, :], stM),
                 waits=[mk["pm_st"]], chain=vch)
            P.op("vector", lambda e: e.tensor_copy(r_q[:, :], stQ),
                 chain=vch)
            P.op("vector", lambda e: e.tensor_mul(r_v[:, :], r_m[:, :], r_m[:, :]),
                 chain=vch)
            P.op("vector", lambda e: e.tensor_sub(r_v[:, :], r_q[:, :], r_v[:, :]),
                 inc=(ve, 1), chain=vch)
            ve_var = ve.mark()
            P.op("scalar", lambda e: e.activation(r_s[:, :], r_v[:, :], AF.Ln,
                                                  bias=c_eps[0:1, :]),
                 waits=[ve_var], chain=sch)
            P.op("scalar", lambda e: e.activation(r_s[:, :], r_s[:, :], AF.Exp,
                                                  scale=c_mhalf[0:1, :]),
                 inc=(se, 1), chain=sch)
            se_rstd = se.mark()
            P.op("vector", lambda e: e.tensor_mul(r_ms[:, :], r_m[:, :], r_s[:, :]),
                 waits=[se_rstd], inc=(ve, 1), chain=vch)
            mk["ve_rows"] = ve.mark()
            bw = [mk["ve_rows"]]
            if prev.get("aff_done"):
                bw.append(prev["aff_done"])
            P.op("tensor",
                 lambda e: e.matmul(sB, ONES_F[0][:, :], r_s[:, :], start=True, stop=True),
                 waits=bw)
            P.op("tensor",
                 lambda e: e.matmul(msB, ONES_F[0][:, :], r_ms[:, :], start=True, stop=True),
                 inc=(pm, 1))
            mk["pm_bc"] = pm.mark()
            return mk

        def ln_stats_a(P, src_fn, stM, stQ, prev_read, sq_war=None,
                       extra_sq_waits=(), sq_eng="gpsimd",
                       rowset=None, sqbuf=None):
            mk = {}
            r_m, r_q, r_v, r_s, r_ms = rowset if rowset else ROWS0
            sq_c = sqbuf if sqbuf is not None else SQ0
            sq_sem = gg if sq_eng == "gpsimd" else se
            for k in range(KD):
                w = list(extra_sq_waits)
                if sq_war is not None:
                    w.append(sq_war)
                if sq_eng == "gpsimd":
                    P.op("gpsimd",
                         lambda e, k=k: e.tensor_mul(sq_c[:, k, :], src_fn(k), src_fn(k)),
                         waits=w if k == 0 else [],
                         inc=(sq_sem, 1) if k == KD - 1 else None)
                else:
                    P.op("scalar",
                         lambda e, k=k: e.activation(sq_c[:, k, :], src_fn(k), AF.Square),
                         waits=w if k == 0 else [],
                         inc=(sq_sem, 1) if k == KD - 1 else None)
            mk["gg_sq"] = sq_sem.mark()
            for k in range(KD):
                w = list(extra_sq_waits)
                if prev_read is not None:
                    w.append(prev_read)
                P.op("tensor",
                     lambda e, k=k: e.matmul(stM, red_bf[:, :], src_fn(k),
                                             start=(k == 0), stop=(k == KD - 1)),
                     waits=w if k == 0 else [])
            for k in range(KD):
                P.op("tensor",
                     lambda e, k=k: e.matmul(stQ, red_bf[:, :], sq_c[:, k, :],
                                             start=(k == 0), stop=(k == KD - 1)),
                     waits=[mk["gg_sq"]] if k == 0 else [],
                     inc=(pm, 1) if k == KD - 1 else None)
            mk["pm_st"] = pm.mark()
            P.op("vector", lambda e: e.tensor_copy(r_m[:, :], stM),
                 waits=[mk["pm_st"]], chain=vch)
            P.op("vector", lambda e: e.tensor_copy(r_q[:, :], stQ),
                 inc=(ve, 1), chain=vch)
            mk["ve_st_read"] = ve.mark()
            P.op("vector", lambda e: e.tensor_mul(r_v[:, :], r_m[:, :], r_m[:, :]),
                 chain=vch)
            P.op("vector", lambda e: e.tensor_sub(r_v[:, :], r_q[:, :], r_v[:, :]),
                 inc=(ve, 1), chain=vch)
            ve_var = ve.mark()
            P.op("scalar", lambda e: e.activation(r_s[:, :], r_v[:, :], AF.Ln,
                                                  bias=c_eps[0:1, :]),
                 waits=[ve_var], chain=sch)
            P.op("scalar", lambda e: e.activation(r_s[:, :], r_s[:, :], AF.Exp,
                                                  scale=c_mhalf[0:1, :]),
                 inc=(se, 1), chain=sch)
            se_rstd = se.mark()
            P.op("vector", lambda e: e.tensor_mul(r_ms[:, :], r_m[:, :], r_s[:, :]),
                 waits=[se_rstd], inc=(ve, 1), chain=vch)
            mk["ve_rows"] = ve.mark()
            mk["_rows"] = (r_s, r_ms)
            mk["_ones"] = ONES_F[1] if rowset is ROWSB else ONES_F[0]
            return mk

        def ln_bcast(P, sB, msB, mk, prev_aff_done=None):
            r_s, r_ms = mk["_rows"]
            ones = mk["_ones"]
            bw = [mk["ve_rows"]]
            if prev_aff_done is not None:
                bw.append(prev_aff_done)
            P.op("tensor",
                 lambda e, r_s=r_s, ones=ones: e.matmul(
                     sB, ones[:, :], r_s[:, :], start=True, stop=True),
                 waits=bw)
            P.op("tensor",
                 lambda e, r_ms=r_ms, ones=ones: e.matmul(
                     msB, ones[:, :], r_ms[:, :], start=True, stop=True),
                 inc=(pm, 1))
            mk["pm_bc"] = pm.mark()
            return mk

        with ExitStack() as attn_scope:
            sba = lambda name, shape, dt: attn_scope.enter_context(
                nc.sbuf_tensor(name, shape, dt))
            xn1T = sba("xn1T", [128, KD, T], BF16)
            xt_tile = sba("xt_tile", [128, 2, D], BF16)
            Kq_s = sba("Kq_s", [128, KD, HD], BF16)
            qw_bf = sba("qw_bf", [128, 32, HD], BF16)
            C_bf = sba("C_bf", [128, 32, HD], BF16)

            rows2 = sba("rows2", [1, 5 * 256], F32)
            ROWS1 = tuple(rows2[:, i * 256:(i + 1) * 256] for i in range(5))
            B_s = sba("B_s", [128, 4, HD], BF16)
            msum_s = sba("msum_s", [128, KD, HD], BF16)
            at_c = sba("at_c", [128, KD, 256], BF16)
            at_c2 = sba("at_c2", [128, KD, 256], BF16)
            asq_c = sba("asq_c", [128, KD, 256], BF16)

            P = Plan(nc, sems)
            P.dma("gpsimd",
                  lambda e: e.dma_start(X[:, :, :],
                                        xT_d.rearrange("(k p) t -> p k t", p=128)),
                  inc=(dX, 16))
            P.dma("gpsimd", lambda e: e.dma_start(idb[:, :], ident_d[:, :]), inc=(wsm, 16))
            P.dma("gpsimd", lambda e: e.dma_start(qw_bf[:, :, :], qwD_d[:, :, :]),
                  inc=(wsm, 16))
            P.dma("gpsimd", lambda e: e.dma_start(C_bf[:, :, :], combD_d[:, :, :]),
                  inc=(wsm, 16))
            P.dma("gpsimd", lambda e: e.dma_start(rw_bf[:, :, :], rwD_d[:, :, :]),
                  inc=(wsm, 16))
            P.dma("sync", lambda e: e.dma_start(idf[:, :], ident_d[:, :]), inc=(wsm, 16))
            P.dma("sync", lambda e: e.dma_start(bias_row[:, :], bias_d[:, :]),
                  inc=(wsm, 16))
            N_SMALL = 6
            for ed in range(NE):
                for srcd, dstd in ((w1T_d, w1s), (w3T_d, w3s), (w2T_d, w2s)):
                    P.dma("gpsimd",
                          lambda e, s=srcd, d=dstd, ed=ed: e.dma_start(d[ed], s[ed]),
                          inc=(wpe[ed], 16))
            P.op("vector", lambda e: e.memset(red_bf[:, :], 1.0 / D))
            P.op("vector", lambda e: e.memset(zw_bf[:, :], 0.0))
            P.op("vector", lambda e: e.memset(ones_r_bf[:, :], 1.0))
            P.op("vector", lambda e: e.memset(ones_r_f[:, :], 1.0))
            P.op("vector", lambda e: e.memset(cst[:, 0:1], 1e-5))
            P.op("vector", lambda e: e.memset(cst[:, 1:2], -0.5))
            P.op("vector", lambda e: e.memset(cst[:, 2:3], -1.0))
            P.emit(no_gpsimd_drain=True)

            with ExitStack() as a1:
                tr_ps = a1.enter_context(nc.psum_tensor("tr_ps", [128, 2, D], BF16))
                gram_ps = a1.enter_context(nc.psum_tensor("gram_ps", [128, KD, HD], F32))
                st_ps = a1.enter_context(nc.psum_tensor("st_ps", [1, 512], F32))
                bc_ps = a1.enter_context(nc.psum_tensor("bc_ps", [128, 512], F32))
                sB = bc_ps[:, 0:256]
                msB = bc_ps[:, 256:512]

                P = Plan(nc, sems)
                gv = gram_ps[:, :, :].rearrange("p a b -> p (a b)")
                for i in range(4):
                    P.op("tensor",
                         lambda e, i=i: e.matmul(gv[:, i * 512:(i + 1) * 512],
                                                 ones_r_bf[:, :], zw_bf[:, :],
                                                 start=True, stop=True),
                         inc=(pm, 1) if i == 3 else None)
                se_ev = {}
                pm_gram = {}
                a1_prev_read = None
                a1_sq_war = {0: None, 1: None}
                a1_aff_prev = None
                a1_mks = {}
                A1_RS = [ROWS0, ROWS1]
                A1_SQ = [SQ0, asq_c]

                def a1_flush(tc):
                    nonlocal a1_aff_prev
                    mk = a1_mks[tc]
                    cs = slice(tc * 256, (tc + 1) * 256)
                    ln_bcast(P, sB, msB, mk, prev_aff_done=a1_aff_prev)
                    for k in range(KD):
                        P.op("vector",
                             lambda e, k=k, cs=cs: e.tensor_mul(
                                 t_aff[:, :], X[:, k, cs], sB),
                             waits=[mk["pm_bc"]] if k == 0 else [])
                        P.op("vector",
                             lambda e, k=k, cs=cs: e.tensor_sub(
                                 xn1T[:, k, cs], t_aff[:, :], msB),
                             inc=(ve, 1) if k == KD - 1 else None)
                    mk["aff_done"] = ve.mark()
                    a1_aff_prev = mk["aff_done"]

                def a1_trgram(tc):
                    mk = a1_mks[tc]
                    for tt2 in range(2):
                        tt = tc * 2 + tt2
                        pr = tt % 2
                        ts = slice(tt * 128, (tt + 1) * 128)
                        twaits = [mk["aff_done"]]
                        if tt >= 2:
                            twaits.append(se_ev[pr])
                        for k in range(KD):
                            P.op("tensor",
                                 lambda e, k=k, ts=ts, pr=pr: e.transpose(
                                     tr_ps[:, pr, k * 128:(k + 1) * 128],
                                     xn1T[:, k, ts], idb[:, :]),
                                 waits=twaits if k == 0 else [],
                                 inc=(pm, 1) if k == KD - 1 else None)
                        pm_tr = pm.mark()
                        sw = [pm_tr]
                        if tt >= 2:
                            sw.append(pm_gram[tt - 2])
                        P.op("scalar",
                             lambda e, pr=pr: e.copy(xt_tile[:, pr, :], tr_ps[:, pr, :]),
                             waits=sw, inc=(se, 1))
                        se_ev[pr] = se.mark()
                        for qq in range(NA):
                            for ft in range(2):
                                P.op("tensor",
                                     lambda e, qq=qq, ft=ft, pr=pr, tt=tt: e.matmul(
                                         gram_ps[:, qq * 2 + ft, :],
                                         xt_tile[:, pr, qq * 256 + ft * 128:
                                                 qq * 256 + (ft + 1) * 128],
                                         xt_tile[:, pr, qq * 256:(qq + 1) * 256],
                                         start=False, stop=(tt == T // 128 - 1),
                                         skip_group_check=True),
                                     waits=[se_ev[pr]] if (qq == 0 and ft == 0) else [],
                                     inc=(pm, 1) if (qq == NA - 1 and ft == 1) else None)
                        pm_gram[tt] = pm.mark()

                for tc in range(T // 256):
                    cs = slice(tc * 256, (tc + 1) * 256)
                    par = tc % 2
                    mk = ln_stats_a(P, lambda k, cs=cs: X[:, k, cs],
                                    st_ps[:, 0:256], st_ps[:, 256:512],
                                    a1_prev_read, sq_war=a1_sq_war[par],
                                    extra_sq_waits=[(dX, 16)], sq_eng="scalar",
                                    rowset=A1_RS[par], sqbuf=A1_SQ[par])
                    a1_prev_read = mk["ve_st_read"]
                    a1_sq_war[par] = mk["pm_st"]
                    a1_mks[tc] = mk
                    if tc >= 1:
                        a1_flush(tc - 1)
                    if tc >= 2:
                        a1_trgram(tc - 2)
                a1_flush(T // 256 - 1)
                a1_trgram(T // 256 - 2)
                a1_trgram(T // 256 - 1)
                last_gram = pm_gram[T // 128 - 1]
                for i in range(KD):
                    P.op("vector",
                         lambda e, i=i: e.tensor_copy(Kq_s[:, i, :], gram_ps[:, i, :]),
                         waits=[last_gram] if i == 0 else [],
                         inc=(ve, 1) if i == KD - 1 else None)
                P.emit(no_gpsimd_drain=True)
            tap("tap_xn1T", lambda: xn1T[:, :, :])
            tap("tap_Kq", lambda: Kq_s[:, :, :])

            with ExitStack() as a2:
                B_ps = a2.enter_context(nc.psum_tensor("B_ps", [128, 2, HD], F32))
                ms_ps0 = a2.enter_context(nc.psum_tensor("ms_ps0", [128, 512], F32))
                ms_ps1 = a2.enter_context(nc.psum_tensor("ms_ps1", [128, 512], F32))
                ms_ps = [ms_ps0, ms_ps1]
                bb_ps = a2.enter_context(nc.psum_tensor("bb_ps", [128, 8], F32))

                P = Plan(nc, sems)
                qcl = (wsm, 16 * N_SMALL)
                P.op("tensor",
                     lambda e: e.matmul(bb_ps[:, :], ONES_F[0][:, :], bias_row[:, :],
                                        start=True, stop=True),
                     waits=[qcl], inc=(pm, 1))
                pm_bb = pm.mark()
                P.op("vector", lambda e: e.tensor_copy(biasB[:, :], bb_ps[:, :]),
                     waits=[pm_bb], inc=(ve, 1))

                Bev = {}
                msev_prev = None
                for a in range(NA):
                    for q in range(NA):
                        pr = q % 2
                        aq = a * NA + q
                        for ft in range(2):
                            for gt in range(2):
                                P.op("tensor",
                                     lambda e, q=q, ft=ft, gt=gt, aq=aq: e.matmul(
                                         B_ps[:, ft, :],
                                         Kq_s[:, q * 2 + gt, ft * 128:(ft + 1) * 128],
                                         C_bf[:, aq * 2 + gt, :],
                                         start=(gt == 0), stop=(gt == 1)),
                                     waits=[qcl] if (aq == 0 and ft == 0 and gt == 0) else [],
                                     inc=(pm, 1) if (ft == 1 and gt == 1) else None)
                        pm_B = pm.mark()
                        for ft in range(2):
                            P.op("vector",
                                 lambda e, ft=ft, pr=pr: e.tensor_copy(
                                     B_s[:, pr * 2 + ft, :], B_ps[:, ft, :]),
                                 waits=[pm_B] if ft == 0 else [],
                                 inc=(ve, 1) if ft == 1 else None)
                        Bev[pr] = ve.mark()
                        for ft in range(2):
                            for et in range(2):
                                w = []
                                if ft == 0 and et == 0:
                                    w.append(Bev[pr])
                                    if q == 0 and msev_prev:
                                        w.append(msev_prev)
                                P.op("tensor",
                                     lambda e, q=q, ft=ft, et=et, aq=aq, pr=pr: e.matmul(
                                         ms_ps[et][:, 0:HD],
                                         qw_bf[:, aq * 2 + ft, et * 128:(et + 1) * 128],
                                         B_s[:, pr * 2 + ft, :],
                                         start=(q == 0 and ft == 0),
                                         stop=(q == NA - 1 and ft == 1)),
                                     waits=w,
                                     inc=(pm, 1) if (q == NA - 1 and ft == 1 and et == 1)
                                     else None)
                    pm_ms = pm.mark()
                    for et in range(2):
                        P.op("vector",
                             lambda e, a=a, et=et: e.tensor_copy(
                                 msum_s[:, a * 2 + et, :], ms_ps[et][:, 0:HD]),
                             waits=[pm_ms] if et == 0 else [],
                             inc=(ve, 1) if et == 1 else None)
                    msev_prev = ve.mark()
                P.emit(no_gpsimd_drain=True)
            tap("tap_msum", lambda: msum_s[:, :, :])

            with ExitStack() as a3:
                o_ps = a3.enter_context(nc.psum_tensor("aout_ps", [128, KD * 256], F32))
                st_ps = a3.enter_context(nc.psum_tensor("ast_ps", [1, 512], F32))
                bc_ps = a3.enter_context(nc.psum_tensor("abc_ps", [128, 512], F32))
                sB = bc_ps[:, 0:256]
                msB = bc_ps[:, 256:512]
                A3_AT = [at_c, at_c2]
                A3_RS = [ROWS0, ROWS1]
                A3_SQ = [SQ0, asq_c]

                P = Plan(nc, sems)
                a3_prev_read = None
                a3_sq_war = {0: None, 1: None}
                a3_aff_prev = None
                a3_mks = {}
                ev_prev = None
                gg_fin_prev = None

                def a3_flush(tc):
                    nonlocal a3_aff_prev, gg_fin_prev
                    mk = a3_mks[tc]
                    atb = A3_AT[tc % 2]
                    cs = slice(tc * 256, (tc + 1) * 256)
                    ln_bcast(P, sB, msB, mk, prev_aff_done=a3_aff_prev)
                    gg_k = {}
                    for k in range(KD):
                        kp = k % 2
                        P.op("vector",
                             lambda e, k=k, atb=atb: e.tensor_mul(
                                 t_aff[:, :], atb[:, k, :], sB),
                             waits=([mk["pm_bc"]] if k == 0 else [])
                             + ([gg_k[k - 2]] if k >= 2 else [])
                             + ([gg_fin_prev] if (k == 0 and gg_fin_prev) else []))
                        P.op("vector",
                             lambda e, kp=kp: e.tensor_sub(
                                 t_af2[:, kp, :], t_aff[:, :], msB),
                             inc=(ve, 1))
                        ve_t2 = ve.mark()
                        P.op("gpsimd",
                             lambda e, k=k, cs=cs, kp=kp: e.tensor_add(
                                 X[:, k, cs], X[:, k, cs], t_af2[:, kp, :]),
                             waits=[ve_t2],
                             inc=(gg, 1))
                        gg_k[k] = gg.mark()
                    mk["aff_ve"] = ve.mark()
                    a3_aff_prev = mk["aff_ve"]
                    gg_fin_prev = gg.mark()

                for tc in range(T // 256):
                    cs = slice(tc * 256, (tc + 1) * 256)
                    par = tc % 2
                    atb = A3_AT[par]
                    mwaits = []
                    if ev_prev is not None:
                        mwaits.append(ev_prev)
                    for a in range(NA):
                        for gt in range(2):
                            dt = a * 2 + gt
                            for et in range(2):
                                P.op("tensor",
                                     lambda e, a=a, gt=gt, et=et, dt=dt, cs=cs: e.matmul(
                                         o_ps[:, dt * 256:(dt + 1) * 256],
                                         msum_s[:, a * 2 + et, gt * 128:(gt + 1) * 128],
                                         xn1T[:, a * 2 + et, cs],
                                         start=(et == 0), stop=(et == 1)),
                                     waits=mwaits if (dt == 0 and et == 0) else [],
                                     inc=(pm, 1) if (dt == KD - 1 and et == 1) else None)
                    pm_out = pm.mark()
                    evw = [pm_out]
                    prev_p = a3_mks.get(tc - 2)
                    if prev_p is not None:
                        evw.append(prev_p["pm_st"])
                        evw.append(prev_p["gg_sq"])
                        evw.append(prev_p["aff_ve"])
                    for k in range(KD):
                        P.op("scalar",
                             lambda e, k=k, atb=atb: e.copy(
                                 atb[:, k, :], o_ps[:, k * 256:(k + 1) * 256]),
                             waits=evw if k == 0 else [],
                             inc=(se, 1) if k == KD - 1 else None)
                    ev_prev = se.mark()
                    mk = ln_stats_a(P, lambda k, atb=atb: atb[:, k, :],
                                    st_ps[:, 0:256], st_ps[:, 256:512],
                                    a3_prev_read, sq_war=a3_sq_war[par],
                                    extra_sq_waits=[ev_prev],
                                    rowset=A3_RS[par], sqbuf=A3_SQ[par])
                    a3_prev_read = mk["ve_st_read"]
                    a3_sq_war[par] = mk["pm_st"]
                    a3_mks[tc] = mk
                    if tc >= 1:
                        a3_flush(tc - 1)
                a3_flush(T // 256 - 1)
                a3_gg_final = gg.mark()
                P.emit(no_gpsimd_drain=True)
            tap("tap_X2", lambda: X[:, :, :])

        with ExitStack() as ffn_scope:
            sbf = lambda name, shape, dt: ffn_scope.enter_context(
                nc.sbuf_tensor(name, shape, dt))
            xn2T = sbf("xn2T", [128, KD, TQ], BF16)
            partial = sbf("partial", [128, KD, TQ], BF16)
            cw_rows = sbf("cw_rows", [8, TQ], BF16)
            cw_flat = sbf("cw_flat", [1, TQ], BF16)
            cwB = sbf("cwB", [128, TQ], BF16)
            lg_s = sbf("lg_s", [128, 8, 8], F32)
            tp_a = sbf("tp_a", [128, 8], F32)
            tp_b = sbf("tp_b", [128, 8], F32)
            tp_c = sbf("tp_c", [128, 8], F32)
            tp_d = sbf("tp_d", [128, 8], F32)
            m1c = sbf("m1c", [128, 4], F32)
            cw_tiles = sbf("cw_tiles", [128, 8, 8], F32)
            fin = sbf("fin", [128, KD, 256], BF16)
            ROWSF = [ROWS0, ROWSB]
            SQF = [SQ0, SQ0]

            grp_seq = 0
            out_seq = 0
            dma_marks = {}
            chunk_gctr = [0]
            consume_gctr = [0]
            chunk_marks = {}

            with ExitStack() as mn:
                sbm = lambda name, shape, dt: mn.enter_context(
                    nc.sbuf_tensor(name, shape, dt))
                w1c = sbm("w1c", [128, 2, KD, 512], BF16)
                w3c = sbm("w3c", [128, 2, KD, 512], BF16)
                w2e = sbm("w2e", [128, NH, D], BF16)
                gbuf = sbm("gbuf", [128, NH, 512], BF16)
                s1b = sbm("s1b", [128, 2, 512], BF16)
                t3b = sbm("t3b", [128, 2, 512], BF16)
                h1a = mn.enter_context(nc.psum_tensor("h1a", [128, 512], F32))
                h3a = mn.enter_context(nc.psum_tensor("h3a", [128, 512], F32))
                h1b = mn.enter_context(nc.psum_tensor("h1b", [128, 512], F32))
                h3b = mn.enter_context(nc.psum_tensor("h3b", [128, 512], F32))
                o_ps = mn.enter_context(nc.psum_tensor("fo_ps", [128, 1024], F32))
                bc_ps = mn.enter_context(nc.psum_tensor("fbc_ps", [128, 512], F32))
                scr_ps = mn.enter_context(nc.psum_tensor("fscr_ps", [128, 512], F32))
                h1 = [h1a, h1b]
                h3 = [h3a, h3b]
                P = Plan(nc, sems)
                silu_done = {}
                t3_done = {}
                g_marks = {}
                g_prev = {}
                gfree_marks = {}
                ot_evac = None
                prev_cwb_pm = None
                evac_bank_mark = {0: None, 1: None}
                ep_sq_last = None
                ct_mark_prev_q = None
                cwf_last = None
                ln_prev_read = None
                ln_sq_war = [None, None]
                ln_aff_prev = None
                ln_par = [0]
                stM = scr_ps[0:1, 0:256]
                stQ = scr_ps[0:1, 256:512]
                sB = bc_ps[:, 0:256]
                msB = bc_ps[:, 256:512]
                ct_ps = scr_ps[0:8, 0:128]
                xn2_war_half = {0: None, 1: None}

                def ln2_flush(st):
                    nonlocal ln_aff_prev
                    mk = st["pend"].pop(0)
                    tcc = mk["_tcc"]
                    qq = st["q"]
                    cs = slice(qq * TQ + tcc * 256, qq * TQ + (tcc + 1) * 256)
                    ls = slice(tcc * 256, (tcc + 1) * 256)
                    ln_bcast(P, sB, msB, mk, prev_aff_done=ln_aff_prev)
                    war = xn2_war_half[tcc // 2]
                    for k in range(KD):
                        P.op("vector",
                                lambda e, k=k, cs=cs: e.tensor_mul(
                                    t_aff[:, :], X[:, k, cs], sB),
                                waits=[mk["pm_bc"]] if k == 0 else [])
                        P.op("vector",
                                lambda e, k=k, ls=ls: e.tensor_sub(
                                    xn2T[:, k, ls], t_aff[:, :], msB),
                                waits=[war] if (k == 0 and war) else [],
                                inc=(ve, 1) if k == KD - 1 else None)
                    mk["aff_done"] = ve.mark()
                    ln_aff_prev = mk["aff_done"]
                    if tcc % 2 == 1:
                        st["aff_half"][tcc // 2] = ve.mark()

                def plan_ln2(qq, flush_all=False):
                    nonlocal ln_prev_read
                    st = {"q": qq, "aff_half": {}, "pend": []}
                    for tcc in range(TQ // 256):
                        cs = slice(qq * TQ + tcc * 256, qq * TQ + (tcc + 1) * 256)
                        par = ln_par[0] % 2
                        ln_par[0] += 1
                        mk = ln_stats_a(P, lambda k, cs=cs: X[:, k, cs],
                                        stM, stQ, ln_prev_read,
                                        sq_war=ln_sq_war[par],
                                        extra_sq_waits=[a3_gg_final]
                                        if (qq == 0 and tcc == 0) else [],
                                        rowset=ROWSF[par], sqbuf=SQF[par])
                        ln_prev_read = mk["ve_st_read"]
                        ln_sq_war[0] = ln_sq_war[1] = mk["pm_st"]
                        mk["_tcc"] = tcc
                        st["pend"].append(mk)
                        if tcc >= 1 and (flush_all or tcc <= 2):
                            ln2_flush(st)
                    if flush_all:
                        while st["pend"]:
                            ln2_flush(st)
                    return st

                def plan_router(st):
                    while st["pend"]:
                        ln2_flush(st)
                    ve_xn2 = ln_aff_prev
                    se_wa_prev = None
                    chain_tt = []
                    lgw0 = [ve_xn2]
                    for p_ in (0, 1):
                        if silu_done.get(p_):
                            lgw0.append(silu_done[p_])
                        if t3_done.get(p_):
                            lgw0.append(t3_done[p_])
                    bank_list = [h1a, h3a, h1b, h3b]
                    pm_lg_bank = []
                    for b_i, bank in enumerate(bank_list):
                        for sl in range(2):
                            tt = b_i * 2 + sl
                            tls = slice(tt * 128, (tt + 1) * 128)
                            for k in range(KD):
                                first = (sl == 0 and k == 0)
                                last = (sl == 1 and k == KD - 1)
                                P.op("tensor",
                                     lambda e, k=k, tls=tls, sl=sl, bank=bank,
                                     first=first, last=last: e.matmul(
                                         bank[:, 496 + sl * 8:504 + sl * 8],
                                         xn2T[:, k, tls], rw_bf[:, k, :],
                                         start=first, stop=last),
                                     waits=(lgw0 if (b_i == 0 and first) else []),
                                     inc=(pm, 1) if last else None)
                        pm_lg_bank.append(pm.mark())
                    lg_add_v = []
                    for tt in range(TQ // 128):
                        b_i, sl = divmod(tt, 2)
                        P.op("vector",
                             lambda e, tt=tt, sl=sl, bank=bank_list[tt // 2]:
                             e.tensor_add(
                                 lg_s[:, tt, :], bank[:, 496 + sl * 8:504 + sl * 8],
                                 biasB[:, :]),
                             waits=[pm_lg_bank[b_i]], inc=(ve, 1), chain=vch)
                        lg_add_v.append(ve.mark())
                    st["ve_lg_last"] = lg_add_v[-1]

                def plan_chains(st):
                    se_wa_prev = None
                    chain_tt = []
                    for tt in range(TQ // 128):
                        V = lambda fn, w=(), inc=None: P.op("vector", fn, waits=w, inc=inc, chain=vch)
                        V(lambda e, tt=tt: e.reduce_max(m1c[:, 0:1], lg_s[:, tt, :], axis=AX.X))
                        V(lambda e, tt=tt: e.tensor_scalar(tp_a[:, :], lg_s[:, tt, :], m1c[:, 0:1], None, ALU.is_ge))
                        V(lambda e: e.tensor_copy(tp_b[:, 0:1], tp_a[:, 0:1]))
                        V(lambda e: e.tensor_add(tp_b[:, 1:8], tp_a[:, 1:8], tp_a[:, 0:7]))
                        V(lambda e: e.tensor_copy(tp_c[:, 0:2], tp_b[:, 0:2]))
                        V(lambda e: e.tensor_add(tp_c[:, 2:8], tp_b[:, 2:8], tp_b[:, 0:6]))
                        V(lambda e: e.tensor_copy(tp_d[:, 0:4], tp_c[:, 0:4]))
                        V(lambda e: e.tensor_add(tp_d[:, 4:8], tp_c[:, 4:8], tp_c[:, 0:4]))
                        V(lambda e: e.tensor_sub(tp_d[:, :], tp_d[:, :], tp_a[:, :]))
                        V(lambda e: e.tensor_scalar(tp_d[:, :], tp_d[:, :], 0.5, None, ALU.is_lt))
                        V(lambda e: e.tensor_mul(tp_a[:, :], tp_a[:, :], tp_d[:, :]))
                        V(lambda e, tt=tt: e.scalar_tensor_tensor(
                            tp_b[:, :], tp_a[:, :], -1e30, lg_s[:, tt, :], ALU.mult, ALU.add))
                        V(lambda e: e.reduce_max(m1c[:, 1:2], tp_b[:, :], axis=AX.X))
                        V(lambda e: e.tensor_scalar(tp_c[:, :], tp_b[:, :], m1c[:, 1:2], None, ALU.is_ge))
                        V(lambda e: e.tensor_copy(tp_d[:, 0:1], tp_c[:, 0:1]))
                        V(lambda e: e.tensor_add(tp_d[:, 1:8], tp_c[:, 1:8], tp_c[:, 0:7]))
                        V(lambda e: e.tensor_copy(tp_b[:, 0:2], tp_d[:, 0:2]))
                        V(lambda e: e.tensor_add(tp_b[:, 2:8], tp_d[:, 2:8], tp_d[:, 0:6]))
                        V(lambda e: e.tensor_copy(tp_d[:, 0:4], tp_b[:, 0:4]))
                        V(lambda e: e.tensor_add(tp_d[:, 4:8], tp_b[:, 4:8], tp_b[:, 0:4]))
                        V(lambda e: e.tensor_sub(tp_d[:, :], tp_d[:, :], tp_c[:, :]))
                        V(lambda e: e.tensor_scalar(tp_d[:, :], tp_d[:, :], 0.5, None, ALU.is_lt))
                        V(lambda e: e.tensor_mul(tp_c[:, :], tp_c[:, :], tp_d[:, :]))
                        V(lambda e: e.tensor_sub(m1c[:, 2:3], m1c[:, 0:1], m1c[:, 1:2]),
                          w=([se_wa_prev] if se_wa_prev else []), inc=(ve, 1))
                        ve_d = ve.mark()
                        P.op("scalar", lambda e: e.activation(m1c[:, 3:4], m1c[:, 2:3],
                                                              AF.Sigmoid, scale=c_mone),
                             waits=[ve_d], inc=(se, 1), chain=sch)
                        se_wb = se.mark()
                        P.op("scalar", lambda e: e.activation(m1c[:, 2:3], m1c[:, 2:3],
                                                              AF.Sigmoid),
                             inc=(se, 1), chain=sch)
                        se_wa = se.mark()
                        se_wa_prev = se_wa
                        V(lambda e: e.tensor_scalar(
                            tp_a[:, :], tp_a[:, :], m1c[:, 2:3], None, ALU.mult),
                          w=[se_wa])
                        V(lambda e, tt=tt: e.scalar_tensor_tensor(
                            cw_tiles[:, tt, :], tp_c[:, :], m1c[:, 3:4], tp_a[:, :],
                            ALU.mult, ALU.add),
                          w=[se_wb] + ([ct_mark_prev_q] if ct_mark_prev_q else []),
                          inc=(ve, 1))
                        chain_tt.append(ve.mark())
                    st["chain_tt"] = chain_tt

                def plan_cts(st):
                    nonlocal ct_mark_prev_q, cwf_last
                    chain_tt = st["chain_tt"]
                    ctv = None
                    for tt in range(TQ // 128):
                        tls = slice(tt * 128, (tt + 1) * 128)
                        tw = [chain_tt[tt], (wsm, 16 * N_SMALL)]
                        if tt == 0:
                            tw.append(ln_prev_read)
                            if cwf_last:
                                tw.append(cwf_last)
                        if ctv:
                            tw.append(ctv)
                        P.op("tensor",
                             lambda e, tt=tt: e.transpose(
                                 ct_ps, cw_tiles[:, tt, :], idf[:, :]),
                             waits=tw, inc=(pm, 1))
                        pm_ct = pm.mark()
                        P.op("vector",
                             lambda e, tls=tls: e.tensor_copy(cw_rows[:, tls], ct_ps),
                             waits=[pm_ct], inc=(ve, 1))
                        ctv = ve.mark()
                    ct_mark_prev_q = pm.mark()
                    return ctv

                pro_st = plan_ln2(0, flush_all=True)
                plan_router(pro_st)
                plan_chains(pro_st)
                nxt_st = None
                for q in range(NQ):
                    aff_half = pro_st["aff_half"]
                    ve_lg_last = pro_st["ve_lg_last"]
                    pro_free = None
                    ct_planned = [False]
                    for tc in range(2):
                        tls = slice(tc * 512, (tc + 1) * 512)
                        for e_i in range(NE):
                            routed = e_i >= 2
                            er = e_i - 2
                            if routed:
                                fw = []
                                if prev_cwb_pm is not None:
                                    fw.append(prev_cwb_pm)
                                if pro_free is not None:
                                    fw.append(pro_free)
                                P.dma("sync",
                                      lambda e, er=er: e.dma_start(
                                          cw_flat[0:1, :], cw_rows[er:er + 1, :]),
                                      waits=fw, inc=(wsm, 16))
                                cwf_done = wsm.mark()
                                cwf_last = cwf_done
                            def plan_chunk_dma(c, e_i=e_i):
                                m = chunk_gctr[0]
                                cb = m % 2
                                wlp = wl0 if cb == 0 else wl1
                                cw_waits = [(wpe[e_i], 48)]
                                if m >= 2:
                                    cw_waits.append((wfr, m - 1))
                                P.dma("sync",
                                      lambda e, e_i=e_i, c=c, cb=cb: e.dma_start(
                                          w1c[:, cb, :, :],
                                          w1s[e_i, :, c * 512:(c + 1) * 512]
                                          .rearrange("(k p) h -> p k h", p=128)),
                                      waits=cw_waits, inc=(wlp, 16))
                                P.dma("sync",
                                      lambda e, e_i=e_i, c=c, cb=cb: e.dma_start(
                                          w3c[:, cb, :, :],
                                          w3s[e_i, :, c * 512:(c + 1) * 512]
                                          .rearrange("(k p) h -> p k h", p=128)),
                                      inc=(wlp, 16))
                                chunk_marks[m] = wlp.mark()
                                chunk_gctr[0] += 1

                            plan_chunk_dma(0)
                            plan_chunk_dma(1)
                            plan_chunk_dma(2)
                            plan_chunk_dma(3)
                            w2w = [(wpe[e_i], 48)]
                            if grp_seq > 0:
                                w2w.append((w2fr, grp_seq))
                            for j in range(4):
                                P.dma("sync",
                                      lambda e, e_i=e_i, j=j: e.dma_start(
                                          w2e[:, j * 4:(j + 1) * 4, :],
                                          w2s[e_i, j * 512:(j + 1) * 512, :]
                                          .rearrange("(t p) d -> p t d", p=128)),
                                      waits=w2w if j == 0 else [],
                                      inc=(wl2, 16))
                            wl2_e = wl2.mark()
                            grp_seq += 1

                            if routed:
                                cww = [cwf_done]
                                if silu_done.get(0):
                                    cww.append(silu_done[0])
                                if t3_done.get(0):
                                    cww.append(t3_done[0])
                                P.op("tensor",
                                     lambda e: e.matmul(
                                         h1a[:, :], ones_r_bf[:, :],
                                         cw_flat[0:1, 0:512],
                                         start=True, stop=True),
                                     waits=cww)
                                P.op("tensor",
                                     lambda e: e.matmul(
                                         h3a[:, :], ones_r_bf[:, :],
                                         cw_flat[0:1, 512:TQ],
                                         start=True, stop=True),
                                     inc=(pm, 1))
                                pm_cwb = pm.mark()
                                prev_cwb_pm = pm_cwb
                                P.op("vector",
                                     lambda e: e.tensor_copy(cwB[:, 0:512], h1a[:, :]),
                                     waits=[pm_cwb])
                                P.op("vector",
                                     lambda e: e.tensor_copy(cwB[:, 512:TQ], h3a[:, :]),
                                     inc=(ve, 1))
                                cwb_ev = ve.mark()
                                silu_done[0] = cwb_ev
                                t3_done[0] = cwb_ev

                            for c in range(4):
                                m = consume_gctr[0]
                                cb = m % 2
                                wl_c = chunk_marks[m]
                                for hcc in range(4):
                                    hc = 4 * c + hcc
                                    p_ = hc % 2
                                    hw = [wl_c]
                                    if tc == 0 and e_i == 0 and hc in (0, 1):
                                        if hc == 0:
                                            hw.append(aff_half[0])
                                        hw.append(ve_lg_last)
                                    if tc == 1 and e_i == 0 and hc == 0:
                                        hw.append(aff_half[1])
                                    if silu_done.get(p_):
                                        hw.append(silu_done[p_])
                                    if t3_done.get(p_):
                                        hw.append(t3_done[p_])
                                    for k in range(KD):
                                        P.op("tensor",
                                             lambda e, k=k, hcc=hcc, p_=p_, cb=cb, tls=tls: e.matmul(
                                                 h1[p_][:, :],
                                                 w1c[:, cb, k, hcc * 128:(hcc + 1) * 128],
                                                 xn2T[:, k, tls],
                                                 start=(k == 0), stop=(k == KD - 1)),
                                             waits=hw if k == 0 else [],
                                             inc=(p1, 1) if k == KD - 1 else None)
                                    p1_hc = p1.mark()
                                    lastchunk = (hcc == 3)
                                    for k in range(KD):
                                        incs = []
                                        if k == KD - 1:
                                            incs.append((p3, 1))
                                            if lastchunk:
                                                incs.append((wfr, 1))
                                        P.op("tensor",
                                             lambda e, k=k, hcc=hcc, p_=p_, cb=cb, tls=tls: e.matmul(
                                                 h3[p_][:, :],
                                                 w3c[:, cb, k, hcc * 128:(hcc + 1) * 128],
                                                 xn2T[:, k, tls],
                                                 start=(k == 0), stop=(k == KD - 1)),
                                             incs=incs)
                                    p3_hc = p3.mark()
                                    sw = [p1_hc]
                                    vw = [p3_hc]
                                    if g_prev.get(p_):
                                        sw.append(g_prev[p_])
                                        vw.append(g_prev[p_])
                                    P.op("scalar",
                                         lambda e, p_=p_: e.activation(
                                             s1b[:, p_, :], h1[p_][:, :], AF.Silu),
                                         waits=sw, inc=(se, 1))
                                    silu_done[p_] = se.mark()
                                    if routed:
                                        P.op("vector",
                                             lambda e, p_=p_, tls=tls: e.tensor_mul(
                                                 t3b[:, p_, :], h3[p_][:, :], cwB[:, tls]),
                                             waits=vw, inc=(ve, 1))
                                        t3_done[p_] = ve.mark()
                                    else:
                                        P.op("scalar",
                                             lambda e, p_=p_: e.copy(
                                                 t3b[:, p_, :], h3[p_][:, :]),
                                             waits=vw, inc=(se, 1))
                                        t3_done[p_] = se.mark()
                                    gw = [silu_done[p_], t3_done[p_]]
                                    if hc in gfree_marks:
                                        gw.append(gfree_marks[hc])
                                    P.op("gpsimd",
                                         lambda e, hc=hc, p_=p_: e.tensor_mul(
                                             gbuf[:, hc, :], s1b[:, p_, :], t3b[:, p_, :]),
                                         waits=gw, inc=(gg, 1))
                                    g_marks[hc] = gg.mark()
                                    g_prev[p_] = g_marks[hc]
                                consume_gctr[0] += 1

                            if tc == 0 and e_i == 1 and not ct_planned[0]:
                                pro_free = plan_cts(pro_st)
                                ct_planned[0] = True

                            col0 = tc * 512
                            for dt in range(KD):
                                bk = dt % 2
                                ow = [wl2_e]
                                if evac_bank_mark[bk] is not None:
                                    ow.append(evac_bank_mark[bk])
                                for hc in range(NH):
                                    incs = []
                                    if hc == NH - 1:
                                        incs.append((po, 1))
                                        if dt == KD - 1:
                                            incs.append((w2fr, 1))
                                    if dt == KD - 1:
                                        incs.append((gf, 1))
                                    w = []
                                    if dt == 0:
                                        w.append(g_marks[hc])
                                    if hc == 0:
                                        w.extend(ow)
                                    P.op("tensor",
                                         lambda e, hc=hc, dt=dt, bk=bk: e.matmul(
                                             o_ps[:, bk * 512:(bk + 1) * 512],
                                             w2e[:, hc, dt * 128:(dt + 1) * 128],
                                             gbuf[:, hc, :],
                                             start=(hc == 0), stop=(hc == NH - 1)),
                                         waits=w, incs=incs)
                                    if dt == KD - 1:
                                        gfree_marks[hc] = gf.mark()
                                po_m = po.mark()
                                if e_i == 0:
                                    P.op("vector",
                                         lambda e, dt=dt, bk=bk, col0=col0: e.tensor_copy(
                                             partial[:, dt, col0:col0 + 512],
                                             o_ps[:, bk * 512:(bk + 1) * 512]),
                                         waits=[po_m] + ([ep_sq_last] if
                                                         (tc == 0 and dt == 0 and ep_sq_last)
                                                         else []),
                                         inc=(ve, 1))
                                else:
                                    P.op("vector",
                                         lambda e, dt=dt, bk=bk, col0=col0: e.tensor_add(
                                             partial[:, dt, col0:col0 + 512],
                                             o_ps[:, bk * 512:(bk + 1) * 512],
                                             partial[:, dt, col0:col0 + 512]),
                                         waits=[po_m], inc=(ve, 1))
                                evac_bank_mark[bk] = ve.mark()
                            ot_evac = ve.mark()

                            if tc == 1 and e_i == 2 and q + 1 < NQ:
                                nxt_st = plan_ln2(q + 1)

                        xn2_war_half[tc] = p3.mark()

                        if tc == 1 and q + 1 < NQ:
                            while nxt_st["pend"]:
                                ln2_flush(nxt_st)
                        ep_mks = []
                        for tcc in (2 * tc, 2 * tc + 1):
                            ls = slice(tcc * 256, (tcc + 1) * 256)
                            par = ln_par[0] % 2
                            ln_par[0] += 1
                            epw = [ot_evac]
                            if pro_free is not None:
                                epw.append(pro_free)
                            mk = ln_stats_a(P, lambda k, ls=ls: partial[:, k, ls],
                                            stM, stQ, ln_prev_read,
                                            sq_war=ln_sq_war[par],
                                            extra_sq_waits=epw,
                                            rowset=ROWSF[par], sqbuf=SQF[par])
                            ln_prev_read = mk["ve_st_read"]
                            ln_sq_war[0] = ln_sq_war[1] = mk["pm_st"]
                            mk["_tcc"] = tcc
                            ep_mks.append(mk)
                        if tc == 1 and q + 1 < NQ:
                            plan_router(nxt_st)
                        for mk in ep_mks:
                            tcc = mk["_tcc"]
                            ls = slice(tcc * 256, (tcc + 1) * 256)
                            cs = slice(q * TQ + tcc * 256, q * TQ + (tcc + 1) * 256)
                            osp = os0 if out_seq % 2 == 0 else os1
                            ln_bcast(P, sB, msB, mk, prev_aff_done=ln_aff_prev)
                            gg_k = {}
                            for k in range(KD):
                                kp = k % 2
                                P.op("vector",
                                     lambda e, k=k, ls=ls: e.tensor_mul(
                                         t_aff[:, :], partial[:, k, ls], sB),
                                     waits=([mk["pm_bc"]] if k == 0 else [])
                                     + ([gg_k[k - 2]] if k >= 2 else []))
                                P.op("vector",
                                     lambda e, kp=kp: e.tensor_sub(
                                         t_af2[:, kp, :], t_aff[:, :], msB),
                                     inc=(ve, 1))
                                ve_t2 = ve.mark()
                                gw = [ve_t2]
                                if k == 0 and out_seq >= 1:
                                    gw.append(dma_marks[out_seq - 1])
                                P.op("gpsimd",
                                     lambda e, k=k, cs=cs, kp=kp: e.tensor_add(
                                         fin[:, k, :], X[:, k, cs], t_af2[:, kp, :]),
                                     waits=gw, inc=(gg, 1))
                                gg_k[k] = gg.mark()
                            mk["aff_done"] = ve.mark()
                            ln_aff_prev = mk["aff_done"]
                            gg_fin = gg.mark()
                            P.dma("gpsimd",
                                  lambda e, cs=cs: e.dma_start(
                                      outT_d.rearrange("(k p) t -> p k t", p=128)[:, :, cs],
                                      fin[:, :, :]),
                                  waits=[gg_fin], inc=(osp, 16))
                            dma_marks[out_seq] = osp.mark()
                            out_seq += 1
                        if tc == 1:
                            if q + 1 < NQ:
                                plan_chains(nxt_st)
                            ep_sq_last = ep_mks[-1].get("gg_sq")
                    if DEBUG and q == 0:
                        P.emit(no_gpsimd_drain=True)
                        tap("tap_cw", lambda: cw_rows[:, :])
                        tap("tap_xn2T", lambda: xn2T[:, :, :])
                        tap("tap_partial", lambda: partial[:, :, :])
                        P = Plan(nc, sems)
                    pro_st = nxt_st
                P.emit(no_gpsimd_drain=True)

        P = Plan(nc, sems)
        P.op("sync", lambda e: e.nop(), waits=[(os0, os0.n), (os1, os1.n)])
        P.emit(no_gpsimd_drain=True)

    return nc


_NC_CACHE = {}


def _get_nc():
    if "nc" not in _NC_CACHE:
        _NC_CACHE["nc"] = build_nc()
    return _NC_CACHE["nc"]


def kernel(L0_attn_hidden_states, L0_attn_queries, L0_attn_combiners,
           L0_ffn_router_w, L0_ffn_expert_bias, L0_ffn_w1, L0_ffn_w3,
           L0_ffn_w2, L0_ffn_sw1, L0_ffn_sw3, L0_ffn_sw2):
    hs = np.asarray(L0_attn_hidden_states, np.float32)
    queries = np.asarray(L0_attn_queries, np.float32)
    comb = np.asarray(L0_attn_combiners, np.float32)
    rw = np.asarray(L0_ffn_router_w, np.float32)
    bias = np.asarray(L0_ffn_expert_bias, np.float32)
    w1 = np.asarray(L0_ffn_w1, np.float32)
    w3 = np.asarray(L0_ffn_w3, np.float32)
    w2 = np.asarray(L0_ffn_w2, np.float32)
    sw1 = np.asarray(L0_ffn_sw1, np.float32)
    sw3 = np.asarray(L0_ffn_sw3, np.float32)
    sw2 = np.asarray(L0_ffn_sw2, np.float32)

    w1T = np.ascontiguousarray(
        np.concatenate([sw1, w1], axis=0).transpose(0, 2, 1))
    w3T = np.ascontiguousarray(
        np.concatenate([sw3, w3], axis=0).transpose(0, 2, 1))
    w2T = np.ascontiguousarray(
        np.concatenate([sw2, w2], axis=0).transpose(0, 2, 1))
    qwD = np.ascontiguousarray(
        queries.reshape(NA, HD, NA, 2, 128).transpose(4, 0, 2, 3, 1)
        .reshape(128, 32, HD))
    combD = np.ascontiguousarray(
        comb.reshape(NA, NA, 2, 128, HD).transpose(3, 0, 1, 2, 4)
        .reshape(128, 32, HD))
    rwD = np.ascontiguousarray(
        rw.T.reshape(KD, 128, 8).transpose(1, 0, 2))
    biasR = np.ascontiguousarray(bias.reshape(1, 8))

    nc = _get_nc()
    in_maps = []
    for b in range(NCORES):
        in_maps.append({
            "xT": np.ascontiguousarray(hs[b].T),
            "w1T": w1T, "w3T": w3T, "w2T": w2T,
            "qwD": qwD, "combD": combD, "rwD": rwD, "bias": biasR,
        })
    res = run_bass_kernel_spmd(nc, in_maps, core_ids=list(range(NCORES)))
    out = np.stack([res.results[b]["outT"].T for b in range(NCORES)], axis=0)
    return np.ascontiguousarray(out.astype(np.float32))


# version 36
# speedup vs baseline: 1.1201x; 1.1201x over previous
import numpy as np
from contextlib import ExitStack

import concourse.bass as bass
import concourse.mybir as mybir
from concourse.bass_utils import run_bass_kernel_spmd

F32 = mybir.dt.float32
BF16 = mybir.dt.bfloat16
AF = mybir.ActivationFunctionType
ALU = mybir.AluOpType
AX = mybir.AxisListType

NCORES = 8
T = 4096
D = 1024
KD = 8
H = 2048
NH = 16
NE = 10
NQ = 4
TQ = T // NQ
NA = 4
HD = 256

ENGINES = ("sync", "tensor", "vector", "scalar", "gpsimd")


class S:

    def __init__(self, stack, nc, name, is_dma=False, no_barrier=False):
        self.h = stack.enter_context(nc.semaphore(name))
        self.name = name
        self.is_dma = is_dma or no_barrier
        self.n = 0
        self.barrier = 0

    def mark(self):
        return (self, self.n)


_TRACE = None
DEBUG = False


class Plan:

    _block_ctr = [0]

    def __init__(self, nc, sems):
        self.nc = nc
        self.sems = sems
        self.ops = {e: [] for e in ENGINES}
        self.waited = {e: {} for e in ENGINES}
        self.block_id = Plan._block_ctr[0]
        Plan._block_ctr[0] += 1

    def op(self, eng, fn, waits=(), inc=None, incs=(), chain=None):
        all_incs = list(incs)
        if inc is not None:
            all_incs.append(inc)
        if chain is not None:
            waits = list(waits) + [(chain, chain.n)]
            all_incs.append((chain, 1))
        wl = []
        for s, v in waits:
            if v <= 0:
                continue
            if (not s.is_dma) and v <= s.barrier:
                continue
            prev = self.waited[eng].get(s.name, -1)
            if v <= prev:
                continue
            self.waited[eng][s.name] = v
            wl.append((s, v))
        self.ops[eng].append((fn, wl, all_incs))
        if _TRACE is not None:
            _TRACE.append((self.block_id, eng,
                           [(s.name, v) for s, v in wl],
                           [(s.name, k) for s, k in all_incs]))
        for s, k in all_incs:
            s.n += k

    dma = op

    def emit(self, no_gpsimd_drain=False):
        plan = self

        def mk(engname):
            def section(eng):
                for fn, wl, incs in plan.ops[engname]:
                    for s, v in wl:
                        eng.wait_ge(s.h, v)
                    inst = fn(eng)
                    if incs:
                        inst.then_inc(incs[0][0].h, incs[0][1])
                        for s, k in incs[1:]:
                            eng.nop().then_inc(s.h, k)
            return section

        with self.nc.Block(no_gpsimd_drain=no_gpsimd_drain) as blk:
            for engname in ENGINES:
                if self.ops[engname]:
                    getattr(blk, engname)(mk(engname))
        for s in self.sems.values():
            if not s.is_dma:
                s.barrier = s.n


def build_nc():
    nc = bass.Bass("TRN2", target_bir_lowering=False)

    xT_d = nc.dram_tensor("xT", [D, T], F32, kind="ExternalInput")
    w1T_d = nc.dram_tensor("w1T", [NE, D, H], F32, kind="ExternalInput")
    w3T_d = nc.dram_tensor("w3T", [NE, D, H], F32, kind="ExternalInput")
    w2T_d = nc.dram_tensor("w2T", [NE, H, D], F32, kind="ExternalInput")
    qwD_d = nc.dram_tensor("qwD", [128, 32, HD], F32, kind="ExternalInput")
    combD_d = nc.dram_tensor("combD", [128, 32, HD], F32, kind="ExternalInput")
    rwD_d = nc.dram_tensor("rwD", [128, KD, 8], F32, kind="ExternalInput")
    bias_d = nc.dram_tensor("bias", [1, 8], F32, kind="ExternalInput")
    ident_d = nc.inline_tensor(np.eye(128, dtype=np.float32), "ident")

    outT_d = nc.dram_tensor("outT", [D, T], F32, kind="ExternalOutput")

    taps = {}
    if DEBUG:
        for nm, shape in [("tap_xn1T", [128, KD, T]), ("tap_Kq", [128, KD, HD]),
                          ("tap_msum", [128, KD, HD]), ("tap_X2", [128, KD, T]),
                          ("tap_cw", [8, TQ]), ("tap_xn2T", [128, KD, TQ]),
                          ("tap_partial", [128, KD, TQ])]:
            taps[nm] = nc.dram_tensor(nm, shape, BF16, kind="ExternalOutput")

    w1s = nc.dram_tensor("w1s", [NE, D, H], BF16, kind="Internal")
    w3s = nc.dram_tensor("w3s", [NE, D, H], BF16, kind="Internal")
    w2s = nc.dram_tensor("w2s", [NE, H, D], BF16, kind="Internal")

    g = ExitStack()
    with g:
        sb = lambda name, shape, dt: g.enter_context(nc.sbuf_tensor(name, shape, dt))

        X = sb("X", [128, KD, T], BF16)
        idb = sb("idb", [128, 128], BF16)
        idf = sb("idf", [128, 128], F32)
        red_bf = sb("red_bf", [128, 1], BF16)
        ones_r_bf = sb("ones_r_bf", [1, 128], BF16)
        ones_r_f = sb("ones_r_f", [33, 128], F32)
        ONES_F = [ones_r_f[0:1, :], ones_r_f[32:33, :]]
        zw_bf = sb("zw_bf", [1, 512], BF16)
        cst = sb("cst", [128, 4], F32)
        c_eps = cst[:, 0:1]
        c_mhalf = cst[:, 1:2]
        c_mone = cst[:, 2:3]
        rw_bf = sb("rw_bf", [128, KD, 8], BF16)
        biasB = sb("biasB", [128, 8], F32)
        bias_row = sb("bias_row", [1, 8], F32)
        rows = sb("rows", [33, 5 * 256], F32)
        r_m = rows[0:1, 0:256]
        r_q = rows[0:1, 256:512]
        r_v = rows[0:1, 512:768]
        r_s = rows[0:1, 768:1024]
        r_ms = rows[0:1, 1024:1280]
        ROWS0 = (r_m, r_q, r_v, r_s, r_ms)
        ROWSB = tuple(rows[32:33, i * 256:(i + 1) * 256] for i in range(5))
        sq_c = sb("sq_c", [128, KD, 256], BF16)
        SQ0 = sq_c
        t_aff = sb("t_aff", [128, 256], F32)
        t_af2 = sb("t_af2", [128, 4, 256], F32)

        semstack = g.enter_context(ExitStack())
        sems = {}

        def mksem(name, is_dma=False, no_barrier=False):
            sems[name] = S(semstack, nc, name, is_dma, no_barrier)
            return sems[name]

        dX = mksem("dX", True)
        wsm = mksem("wsm", True)
        wl0 = mksem("wl0", True)
        wl1 = mksem("wl1", True)
        wl2 = mksem("wl2", True)
        os0 = mksem("os0", True)
        os1 = mksem("os1", True)
        wpe = [mksem(f"wpe{i}", True) for i in range(NE)]
        wfr = mksem("wfr")
        w2fr = mksem("w2fr")
        p1 = mksem("p1")
        p3 = mksem("p3")
        po = mksem("po")
        pm = mksem("pm")
        gf = mksem("gf")
        ve = mksem("ve")
        vch = mksem("vch")
        sch = mksem("sch")
        gch = mksem("gch", no_barrier=True)
        se = mksem("se")
        gg = mksem("gg", no_barrier=True)

        def tap(name, src_ap_fn):
            if not DEBUG:
                return
            h = taps[name]
            nd = len(list(h.shape))
            dst = h[(slice(None),) * nd]
            Pt = Plan(nc, sems)
            Pt.dma("sync", lambda e: e.dma_start(dst, src_ap_fn()),
                   inc=(wsm, 16))
            w = wsm.mark()
            Pt.op("sync", lambda e: e.nop(), waits=[w])
            Pt.emit(no_gpsimd_drain=True)

        def ln_stats(P, src_fn, stM, stQ, sB, msB, prev, extra_sq_waits=(),
                     sq_eng="gpsimd", rowset=None, sqbuf=None):
            mk = {}
            r_m, r_q, r_v, r_s, r_ms = rowset if rowset else ROWS0
            sq_c = sqbuf if sqbuf is not None else SQ0
            sq_sem = gg if sq_eng == "gpsimd" else se
            for k in range(KD):
                w = list(extra_sq_waits)
                if prev.get("pm_st"):
                    w.append(prev["pm_st"])
                if sq_eng == "gpsimd":
                    P.op("gpsimd",
                         lambda e, k=k: e.tensor_mul(sq_c[:, k, :], src_fn(k), src_fn(k)),
                         waits=w if k == 0 else [],
                         inc=(sq_sem, 1) if k == KD - 1 else None)
                else:
                    P.op("scalar",
                         lambda e, k=k: e.activation(sq_c[:, k, :], src_fn(k), AF.Square),
                         waits=w if k == 0 else [],
                         inc=(sq_sem, 1) if k == KD - 1 else None)
            mk["gg_sq"] = sq_sem.mark()
            for k in range(KD):
                w = list(extra_sq_waits)
                if prev.get("ve_rows"):
                    w.append(prev["ve_rows"])
                P.op("tensor",
                     lambda e, k=k: e.matmul(stM, red_bf[:, :], src_fn(k),
                                             start=(k == 0), stop=(k == KD - 1)),
                     waits=w if k == 0 else [])
            for k in range(KD):
                P.op("tensor",
                     lambda e, k=k: e.matmul(stQ, red_bf[:, :], sq_c[:, k, :],
                                             start=(k == 0), stop=(k == KD - 1)),
                     waits=[mk["gg_sq"]] if k == 0 else [],
                     inc=(pm, 1) if k == KD - 1 else None)
            mk["pm_st"] = pm.mark()
            P.op("vector", lambda e: e.tensor_copy(r_m[:, :], stM),
                 waits=[mk["pm_st"]], chain=vch)
            P.op("vector", lambda e: e.tensor_copy(r_q[:, :], stQ),
                 chain=vch)
            P.op("vector", lambda e: e.tensor_mul(r_v[:, :], r_m[:, :], r_m[:, :]),
                 chain=vch)
            P.op("vector", lambda e: e.tensor_sub(r_v[:, :], r_q[:, :], r_v[:, :]),
                 inc=(ve, 1), chain=vch)
            ve_var = ve.mark()
            P.op("scalar", lambda e: e.activation(r_s[:, :], r_v[:, :], AF.Ln,
                                                  bias=c_eps[0:1, :]),
                 waits=[ve_var], chain=sch)
            P.op("scalar", lambda e: e.activation(r_s[:, :], r_s[:, :], AF.Exp,
                                                  scale=c_mhalf[0:1, :]),
                 inc=(se, 1), chain=sch)
            se_rstd = se.mark()
            P.op("vector", lambda e: e.tensor_mul(r_ms[:, :], r_m[:, :], r_s[:, :]),
                 waits=[se_rstd], inc=(ve, 1), chain=vch)
            mk["ve_rows"] = ve.mark()
            bw = [mk["ve_rows"]]
            if prev.get("aff_done"):
                bw.append(prev["aff_done"])
            P.op("tensor",
                 lambda e: e.matmul(sB, ONES_F[0][:, :], r_s[:, :], start=True, stop=True),
                 waits=bw)
            P.op("tensor",
                 lambda e: e.matmul(msB, ONES_F[0][:, :], r_ms[:, :], start=True, stop=True),
                 inc=(pm, 1))
            mk["pm_bc"] = pm.mark()
            return mk

        def ln_stats_a(P, src_fn, stM, stQ, prev_read, sq_war=None,
                       extra_sq_waits=(), sq_eng="gpsimd",
                       rowset=None, sqbuf=None):
            mk = {}
            r_m, r_q, r_v, r_s, r_ms = rowset if rowset else ROWS0
            sq_c = sqbuf if sqbuf is not None else SQ0
            sq_sem = gg if sq_eng == "gpsimd" else se
            for k in range(KD):
                w = list(extra_sq_waits)
                if sq_war is not None:
                    w.append(sq_war)
                if sq_eng == "gpsimd":
                    P.op("gpsimd",
                         lambda e, k=k: e.tensor_mul(sq_c[:, k, :], src_fn(k), src_fn(k)),
                         waits=w if k == 0 else [],
                         inc=(sq_sem, 1) if k == KD - 1 else None)
                else:
                    P.op("scalar",
                         lambda e, k=k: e.activation(sq_c[:, k, :], src_fn(k), AF.Square),
                         waits=w if k == 0 else [],
                         inc=(sq_sem, 1) if k == KD - 1 else None)
            mk["gg_sq"] = sq_sem.mark()
            for k in range(KD):
                w = list(extra_sq_waits)
                if prev_read is not None:
                    w.append(prev_read)
                P.op("tensor",
                     lambda e, k=k: e.matmul(stM, red_bf[:, :], src_fn(k),
                                             start=(k == 0), stop=(k == KD - 1)),
                     waits=w if k == 0 else [])
            for k in range(KD):
                P.op("tensor",
                     lambda e, k=k: e.matmul(stQ, red_bf[:, :], sq_c[:, k, :],
                                             start=(k == 0), stop=(k == KD - 1)),
                     waits=[mk["gg_sq"]] if k == 0 else [],
                     inc=(pm, 1) if k == KD - 1 else None)
            mk["pm_st"] = pm.mark()
            P.op("vector", lambda e: e.tensor_copy(r_m[:, :], stM),
                 waits=[mk["pm_st"]], chain=vch)
            P.op("vector", lambda e: e.tensor_copy(r_q[:, :], stQ),
                 inc=(ve, 1), chain=vch)
            mk["ve_st_read"] = ve.mark()
            P.op("vector", lambda e: e.tensor_mul(r_v[:, :], r_m[:, :], r_m[:, :]),
                 chain=vch)
            P.op("vector", lambda e: e.tensor_sub(r_v[:, :], r_q[:, :], r_v[:, :]),
                 inc=(ve, 1), chain=vch)
            ve_var = ve.mark()
            P.op("scalar", lambda e: e.activation(r_s[:, :], r_v[:, :], AF.Ln,
                                                  bias=c_eps[0:1, :]),
                 waits=[ve_var], chain=sch)
            P.op("scalar", lambda e: e.activation(r_s[:, :], r_s[:, :], AF.Exp,
                                                  scale=c_mhalf[0:1, :]),
                 inc=(se, 1), chain=sch)
            se_rstd = se.mark()
            P.op("vector", lambda e: e.tensor_mul(r_ms[:, :], r_m[:, :], r_s[:, :]),
                 waits=[se_rstd], inc=(ve, 1), chain=vch)
            mk["ve_rows"] = ve.mark()
            mk["_rows"] = (r_s, r_ms)
            mk["_ones"] = ONES_F[1] if rowset is ROWSB else ONES_F[0]
            return mk

        def ln_bcast(P, sB, msB, mk, prev_aff_done=None):
            r_s, r_ms = mk["_rows"]
            ones = mk["_ones"]
            bw = [mk["ve_rows"]]
            if prev_aff_done is not None:
                bw.append(prev_aff_done)
            P.op("tensor",
                 lambda e, r_s=r_s, ones=ones: e.matmul(
                     sB, ones[:, :], r_s[:, :], start=True, stop=True),
                 waits=bw)
            P.op("tensor",
                 lambda e, r_ms=r_ms, ones=ones: e.matmul(
                     msB, ones[:, :], r_ms[:, :], start=True, stop=True),
                 inc=(pm, 1))
            mk["pm_bc"] = pm.mark()
            return mk

        with ExitStack() as attn_scope:
            sba = lambda name, shape, dt: attn_scope.enter_context(
                nc.sbuf_tensor(name, shape, dt))
            xn1T = sba("xn1T", [128, KD, T], BF16)
            xt_tile = sba("xt_tile", [128, 2, D], BF16)
            Kq_s = sba("Kq_s", [128, KD, HD], BF16)
            qw_bf = sba("qw_bf", [128, 32, HD], BF16)
            C_bf = sba("C_bf", [128, 32, HD], BF16)

            rows2 = sba("rows2", [1, 5 * 256], F32)
            ROWS1 = tuple(rows2[:, i * 256:(i + 1) * 256] for i in range(5))
            B_s = sba("B_s", [128, 4, HD], BF16)
            msum_s = sba("msum_s", [128, KD, HD], BF16)
            at_c = sba("at_c", [128, KD, 256], BF16)
            at_c2 = sba("at_c2", [128, KD, 256], BF16)
            asq_c = sba("asq_c", [128, KD, 256], BF16)

            P = Plan(nc, sems)
            P.dma("gpsimd",
                  lambda e: e.dma_start(X[:, :, :],
                                        xT_d.rearrange("(k p) t -> p k t", p=128)),
                  inc=(dX, 16))
            P.dma("gpsimd", lambda e: e.dma_start(idb[:, :], ident_d[:, :]), inc=(wsm, 16))
            P.dma("gpsimd", lambda e: e.dma_start(qw_bf[:, :, :], qwD_d[:, :, :]),
                  inc=(wsm, 16))
            P.dma("gpsimd", lambda e: e.dma_start(C_bf[:, :, :], combD_d[:, :, :]),
                  inc=(wsm, 16))
            P.dma("gpsimd", lambda e: e.dma_start(rw_bf[:, :, :], rwD_d[:, :, :]),
                  inc=(wsm, 16))
            P.dma("sync", lambda e: e.dma_start(idf[:, :], ident_d[:, :]), inc=(wsm, 16))
            P.dma("sync", lambda e: e.dma_start(bias_row[:, :], bias_d[:, :]),
                  inc=(wsm, 16))
            N_SMALL = 6
            for ed in range(NE):
                for srcd, dstd in ((w1T_d, w1s), (w3T_d, w3s), (w2T_d, w2s)):
                    P.dma("gpsimd",
                          lambda e, s=srcd, d=dstd, ed=ed: e.dma_start(d[ed], s[ed]),
                          inc=(wpe[ed], 16))
            P.op("vector", lambda e: e.memset(red_bf[:, :], 1.0 / D))
            P.op("vector", lambda e: e.memset(zw_bf[:, :], 0.0))
            P.op("vector", lambda e: e.memset(ones_r_bf[:, :], 1.0))
            P.op("vector", lambda e: e.memset(ones_r_f[:, :], 1.0))
            P.op("vector", lambda e: e.memset(cst[:, 0:1], 1e-5))
            P.op("vector", lambda e: e.memset(cst[:, 1:2], -0.5))
            P.op("vector", lambda e: e.memset(cst[:, 2:3], -1.0))
            P.emit(no_gpsimd_drain=True)

            with ExitStack() as a1:
                tr_ps = a1.enter_context(nc.psum_tensor("tr_ps", [128, 2, D], BF16))
                gram_ps = a1.enter_context(nc.psum_tensor("gram_ps", [128, KD, HD], F32))
                st_ps = a1.enter_context(nc.psum_tensor("st_ps", [1, 512], F32))
                bc_ps = a1.enter_context(nc.psum_tensor("bc_ps", [128, 512], F32))
                sB = bc_ps[:, 0:256]
                msB = bc_ps[:, 256:512]

                P = Plan(nc, sems)
                gv = gram_ps[:, :, :].rearrange("p a b -> p (a b)")
                for i in range(4):
                    P.op("tensor",
                         lambda e, i=i: e.matmul(gv[:, i * 512:(i + 1) * 512],
                                                 ones_r_bf[:, :], zw_bf[:, :],
                                                 start=True, stop=True),
                         inc=(pm, 1) if i == 3 else None)
                se_ev = {}
                pm_gram = {}
                a1_prev_read = None
                a1_sq_war = {0: None, 1: None}
                a1_aff_prev = None
                a1_mks = {}
                A1_RS = [ROWS0, ROWS1]
                A1_SQ = [SQ0, asq_c]

                def a1_flush(tc):
                    nonlocal a1_aff_prev
                    mk = a1_mks[tc]
                    cs = slice(tc * 256, (tc + 1) * 256)
                    ln_bcast(P, sB, msB, mk, prev_aff_done=a1_aff_prev)
                    for k in range(KD):
                        P.op("vector",
                             lambda e, k=k, cs=cs: e.tensor_mul(
                                 t_aff[:, :], X[:, k, cs], sB),
                             waits=[mk["pm_bc"]] if k == 0 else [])
                        P.op("vector",
                             lambda e, k=k, cs=cs: e.tensor_sub(
                                 xn1T[:, k, cs], t_aff[:, :], msB),
                             inc=(ve, 1) if k == KD - 1 else None)
                    mk["aff_done"] = ve.mark()
                    a1_aff_prev = mk["aff_done"]

                def a1_trgram(tc):
                    mk = a1_mks[tc]
                    for tt2 in range(2):
                        tt = tc * 2 + tt2
                        pr = tt % 2
                        ts = slice(tt * 128, (tt + 1) * 128)
                        twaits = [mk["aff_done"]]
                        if tt >= 2:
                            twaits.append(se_ev[pr])
                        for k in range(KD):
                            P.op("tensor",
                                 lambda e, k=k, ts=ts, pr=pr: e.transpose(
                                     tr_ps[:, pr, k * 128:(k + 1) * 128],
                                     xn1T[:, k, ts], idb[:, :]),
                                 waits=twaits if k == 0 else [],
                                 inc=(pm, 1) if k == KD - 1 else None)
                        pm_tr = pm.mark()
                        sw = [pm_tr]
                        if tt >= 2:
                            sw.append(pm_gram[tt - 2])
                        P.op("scalar",
                             lambda e, pr=pr: e.copy(xt_tile[:, pr, :], tr_ps[:, pr, :]),
                             waits=sw, inc=(se, 1))
                        se_ev[pr] = se.mark()
                        for qq in range(NA):
                            for ft in range(2):
                                P.op("tensor",
                                     lambda e, qq=qq, ft=ft, pr=pr, tt=tt: e.matmul(
                                         gram_ps[:, qq * 2 + ft, :],
                                         xt_tile[:, pr, qq * 256 + ft * 128:
                                                 qq * 256 + (ft + 1) * 128],
                                         xt_tile[:, pr, qq * 256:(qq + 1) * 256],
                                         start=False, stop=(tt == T // 128 - 1),
                                         skip_group_check=True),
                                     waits=[se_ev[pr]] if (qq == 0 and ft == 0) else [],
                                     inc=(pm, 1) if (qq == NA - 1 and ft == 1) else None)
                        pm_gram[tt] = pm.mark()

                for tc in range(T // 256):
                    cs = slice(tc * 256, (tc + 1) * 256)
                    par = tc % 2
                    mk = ln_stats_a(P, lambda k, cs=cs: X[:, k, cs],
                                    st_ps[:, 0:256], st_ps[:, 256:512],
                                    a1_prev_read, sq_war=a1_sq_war[par],
                                    extra_sq_waits=[(dX, 16)], sq_eng="scalar",
                                    rowset=A1_RS[par], sqbuf=A1_SQ[par])
                    a1_prev_read = mk["ve_st_read"]
                    a1_sq_war[par] = mk["pm_st"]
                    a1_mks[tc] = mk
                    if tc >= 1:
                        a1_flush(tc - 1)
                    if tc >= 2:
                        a1_trgram(tc - 2)
                a1_flush(T // 256 - 1)
                a1_trgram(T // 256 - 2)
                a1_trgram(T // 256 - 1)
                last_gram = pm_gram[T // 128 - 1]
                for i in range(KD):
                    P.op("vector",
                         lambda e, i=i: e.tensor_copy(Kq_s[:, i, :], gram_ps[:, i, :]),
                         waits=[last_gram] if i == 0 else [],
                         inc=(ve, 1) if i == KD - 1 else None)
                P.emit(no_gpsimd_drain=True)
            tap("tap_xn1T", lambda: xn1T[:, :, :])
            tap("tap_Kq", lambda: Kq_s[:, :, :])

            with ExitStack() as a2:
                B_ps = a2.enter_context(nc.psum_tensor("B_ps", [128, 2, HD], F32))
                ms_ps0 = a2.enter_context(nc.psum_tensor("ms_ps0", [128, 512], F32))
                ms_ps1 = a2.enter_context(nc.psum_tensor("ms_ps1", [128, 512], F32))
                ms_ps = [ms_ps0, ms_ps1]
                bb_ps = a2.enter_context(nc.psum_tensor("bb_ps", [128, 8], F32))

                P = Plan(nc, sems)
                qcl = (wsm, 16 * N_SMALL)
                P.op("tensor",
                     lambda e: e.matmul(bb_ps[:, :], ONES_F[0][:, :], bias_row[:, :],
                                        start=True, stop=True),
                     waits=[qcl], inc=(pm, 1))
                pm_bb = pm.mark()
                P.op("vector", lambda e: e.tensor_copy(biasB[:, :], bb_ps[:, :]),
                     waits=[pm_bb], inc=(ve, 1))

                Bev = {}
                msev_prev = None
                for a in range(NA):
                    for q in range(NA):
                        pr = q % 2
                        aq = a * NA + q
                        for ft in range(2):
                            for gt in range(2):
                                P.op("tensor",
                                     lambda e, q=q, ft=ft, gt=gt, aq=aq: e.matmul(
                                         B_ps[:, ft, :],
                                         Kq_s[:, q * 2 + gt, ft * 128:(ft + 1) * 128],
                                         C_bf[:, aq * 2 + gt, :],
                                         start=(gt == 0), stop=(gt == 1)),
                                     waits=[qcl] if (aq == 0 and ft == 0 and gt == 0) else [],
                                     inc=(pm, 1) if (ft == 1 and gt == 1) else None)
                        pm_B = pm.mark()
                        for ft in range(2):
                            P.op("vector",
                                 lambda e, ft=ft, pr=pr: e.tensor_copy(
                                     B_s[:, pr * 2 + ft, :], B_ps[:, ft, :]),
                                 waits=[pm_B] if ft == 0 else [],
                                 inc=(ve, 1) if ft == 1 else None)
                        Bev[pr] = ve.mark()
                        for ft in range(2):
                            for et in range(2):
                                w = []
                                if ft == 0 and et == 0:
                                    w.append(Bev[pr])
                                    if q == 0 and msev_prev:
                                        w.append(msev_prev)
                                P.op("tensor",
                                     lambda e, q=q, ft=ft, et=et, aq=aq, pr=pr: e.matmul(
                                         ms_ps[et][:, 0:HD],
                                         qw_bf[:, aq * 2 + ft, et * 128:(et + 1) * 128],
                                         B_s[:, pr * 2 + ft, :],
                                         start=(q == 0 and ft == 0),
                                         stop=(q == NA - 1 and ft == 1)),
                                     waits=w,
                                     inc=(pm, 1) if (q == NA - 1 and ft == 1 and et == 1)
                                     else None)
                    pm_ms = pm.mark()
                    for et in range(2):
                        P.op("vector",
                             lambda e, a=a, et=et: e.tensor_copy(
                                 msum_s[:, a * 2 + et, :], ms_ps[et][:, 0:HD]),
                             waits=[pm_ms] if et == 0 else [],
                             inc=(ve, 1) if et == 1 else None)
                    msev_prev = ve.mark()
                P.emit(no_gpsimd_drain=True)
            tap("tap_msum", lambda: msum_s[:, :, :])

            with ExitStack() as a3:
                o_ps = a3.enter_context(nc.psum_tensor("aout_ps", [128, KD * 256], F32))
                st_ps = a3.enter_context(nc.psum_tensor("ast_ps", [1, 512], F32))
                bc_ps = a3.enter_context(nc.psum_tensor("abc_ps", [128, 512], F32))
                sB = bc_ps[:, 0:256]
                msB = bc_ps[:, 256:512]
                A3_AT = [at_c, at_c2]
                A3_RS = [ROWS0, ROWS1]
                A3_SQ = [SQ0, asq_c]

                P = Plan(nc, sems)
                a3_prev_read = None
                a3_sq_war = {0: None, 1: None}
                a3_aff_prev = None
                a3_mks = {}
                ev_prev = None
                gg_fin_prev = None

                def a3_flush(tc):
                    nonlocal a3_aff_prev, gg_fin_prev
                    mk = a3_mks[tc]
                    atb = A3_AT[tc % 2]
                    cs = slice(tc * 256, (tc + 1) * 256)
                    ln_bcast(P, sB, msB, mk, prev_aff_done=a3_aff_prev)
                    gg_k = {}
                    for k in range(KD):
                        kp = k % 4
                        P.op("vector",
                             lambda e, k=k, atb=atb: e.tensor_mul(
                                 t_aff[:, :], atb[:, k, :], sB),
                             waits=([mk["pm_bc"]] if k == 0 else [])
                             + ([gg_k[k - 4]] if k >= 4 else [])
                             + ([gg_fin_prev] if (k == 0 and gg_fin_prev) else []))
                        P.op("vector",
                             lambda e, kp=kp: e.tensor_sub(
                                 t_af2[:, kp, :], t_aff[:, :], msB),
                             inc=(ve, 1))
                        ve_t2 = ve.mark()
                        P.op("gpsimd",
                             lambda e, k=k, cs=cs, kp=kp: e.tensor_add(
                                 X[:, k, cs], X[:, k, cs], t_af2[:, kp, :]),
                             waits=[ve_t2],
                             inc=(gg, 1))
                        gg_k[k] = gg.mark()
                    mk["aff_ve"] = ve.mark()
                    a3_aff_prev = mk["aff_ve"]
                    gg_fin_prev = gg.mark()

                for tc in range(T // 256):
                    cs = slice(tc * 256, (tc + 1) * 256)
                    par = tc % 2
                    atb = A3_AT[par]
                    mwaits = []
                    if ev_prev is not None:
                        mwaits.append(ev_prev)
                    for a in range(NA):
                        for gt in range(2):
                            dt = a * 2 + gt
                            for et in range(2):
                                P.op("tensor",
                                     lambda e, a=a, gt=gt, et=et, dt=dt, cs=cs: e.matmul(
                                         o_ps[:, dt * 256:(dt + 1) * 256],
                                         msum_s[:, a * 2 + et, gt * 128:(gt + 1) * 128],
                                         xn1T[:, a * 2 + et, cs],
                                         start=(et == 0), stop=(et == 1)),
                                     waits=mwaits if (dt == 0 and et == 0) else [],
                                     inc=(pm, 1) if (dt == KD - 1 and et == 1) else None)
                    pm_out = pm.mark()
                    evw = [pm_out]
                    prev_p = a3_mks.get(tc - 2)
                    if prev_p is not None:
                        evw.append(prev_p["pm_st"])
                        evw.append(prev_p["gg_sq"])
                        evw.append(prev_p["aff_ve"])
                    for k in range(KD):
                        P.op("scalar",
                             lambda e, k=k, atb=atb: e.copy(
                                 atb[:, k, :], o_ps[:, k * 256:(k + 1) * 256]),
                             waits=evw if k == 0 else [],
                             inc=(se, 1) if k == KD - 1 else None)
                    ev_prev = se.mark()
                    mk = ln_stats_a(P, lambda k, atb=atb: atb[:, k, :],
                                    st_ps[:, 0:256], st_ps[:, 256:512],
                                    a3_prev_read, sq_war=a3_sq_war[par],
                                    extra_sq_waits=[ev_prev],
                                    rowset=A3_RS[par], sqbuf=A3_SQ[par])
                    a3_prev_read = mk["ve_st_read"]
                    a3_sq_war[par] = mk["pm_st"]
                    a3_mks[tc] = mk
                    if tc >= 1:
                        a3_flush(tc - 1)
                a3_flush(T // 256 - 1)
                a3_gg_final = gg.mark()
                P.emit(no_gpsimd_drain=True)
            tap("tap_X2", lambda: X[:, :, :])

        with ExitStack() as ffn_scope:
            sbf = lambda name, shape, dt: ffn_scope.enter_context(
                nc.sbuf_tensor(name, shape, dt))
            xn2T = sbf("xn2T", [128, KD, TQ], BF16)
            partial = sbf("partial", [128, KD, TQ], BF16)
            cw_rows = sbf("cw_rows", [8, TQ], BF16)
            cw_flat = sbf("cw_flat", [1, TQ], BF16)
            cwB = sbf("cwB", [128, TQ], BF16)
            lg_s = sbf("lg_s", [128, 8, 8], F32)
            tp_a = sbf("tp_a", [128, 8], F32)
            tp_b = sbf("tp_b", [128, 8], F32)
            tp_c = sbf("tp_c", [128, 8], F32)
            tp_d = sbf("tp_d", [128, 8], F32)
            m1c = sbf("m1c", [128, 4], F32)
            cw_tiles = sbf("cw_tiles", [128, 8, 8], F32)
            fin = sbf("fin", [128, KD, 256], BF16)
            ROWSF = [ROWS0, ROWSB]
            SQF = [SQ0, SQ0]

            grp_seq = 0
            out_seq = 0
            dma_marks = {}
            chunk_gctr = [0]
            consume_gctr = [0]
            chunk_marks = {}

            with ExitStack() as mn:
                sbm = lambda name, shape, dt: mn.enter_context(
                    nc.sbuf_tensor(name, shape, dt))
                w1c = sbm("w1c", [128, 2, KD, 512], BF16)
                w3c = sbm("w3c", [128, 2, KD, 512], BF16)
                w2e = sbm("w2e", [128, NH, D], BF16)
                gbuf = sbm("gbuf", [128, NH, 512], BF16)
                s1b = sbm("s1b", [128, 2, 512], BF16)
                t3b = sbm("t3b", [128, 2, 512], BF16)
                h1a = mn.enter_context(nc.psum_tensor("h1a", [128, 512], F32))
                h3a = mn.enter_context(nc.psum_tensor("h3a", [128, 512], F32))
                h1b = mn.enter_context(nc.psum_tensor("h1b", [128, 512], F32))
                h3b = mn.enter_context(nc.psum_tensor("h3b", [128, 512], F32))
                o_ps = mn.enter_context(nc.psum_tensor("fo_ps", [128, 1024], F32))
                bc_ps = mn.enter_context(nc.psum_tensor("fbc_ps", [128, 512], F32))
                scr_ps = mn.enter_context(nc.psum_tensor("fscr_ps", [128, 512], F32))
                h1 = [h1a, h1b]
                h3 = [h3a, h3b]
                P = Plan(nc, sems)
                silu_done = {}
                t3_done = {}
                g_marks = {}
                g_prev = {}
                gfree_marks = {}
                ot_evac = None
                prev_cwb_pm = None
                evac_bank_mark = {0: None, 1: None}
                ep_sq_last = None
                ct_mark_prev_q = None
                cwf_last = None
                ln_prev_read = None
                ln_sq_war = [None, None]
                ln_aff_prev = None
                ln_par = [0]
                stM = scr_ps[0:1, 0:256]
                stQ = scr_ps[0:1, 256:512]
                sB = bc_ps[:, 0:256]
                msB = bc_ps[:, 256:512]
                ct_ps = scr_ps[0:8, 0:128]
                xn2_war_half = {0: None, 1: None}

                def ln2_flush(st):
                    nonlocal ln_aff_prev
                    mk = st["pend"].pop(0)
                    tcc = mk["_tcc"]
                    qq = st["q"]
                    cs = slice(qq * TQ + tcc * 256, qq * TQ + (tcc + 1) * 256)
                    ls = slice(tcc * 256, (tcc + 1) * 256)
                    ln_bcast(P, sB, msB, mk, prev_aff_done=ln_aff_prev)
                    war = xn2_war_half[tcc // 2]
                    for k in range(KD):
                        P.op("vector",
                                lambda e, k=k, cs=cs: e.tensor_mul(
                                    t_aff[:, :], X[:, k, cs], sB),
                                waits=[mk["pm_bc"]] if k == 0 else [])
                        P.op("vector",
                                lambda e, k=k, ls=ls: e.tensor_sub(
                                    xn2T[:, k, ls], t_aff[:, :], msB),
                                waits=[war] if (k == 0 and war) else [],
                                inc=(ve, 1) if k == KD - 1 else None)
                    mk["aff_done"] = ve.mark()
                    ln_aff_prev = mk["aff_done"]
                    if tcc % 2 == 1:
                        st["aff_half"][tcc // 2] = ve.mark()

                def plan_ln2(qq, flush_all=False):
                    nonlocal ln_prev_read
                    st = {"q": qq, "aff_half": {}, "pend": []}
                    for tcc in range(TQ // 256):
                        cs = slice(qq * TQ + tcc * 256, qq * TQ + (tcc + 1) * 256)
                        par = ln_par[0] % 2
                        ln_par[0] += 1
                        mk = ln_stats_a(P, lambda k, cs=cs: X[:, k, cs],
                                        stM, stQ, ln_prev_read,
                                        sq_war=ln_sq_war[par],
                                        extra_sq_waits=[a3_gg_final]
                                        if (qq == 0 and tcc == 0) else [],
                                        rowset=ROWSF[par], sqbuf=SQF[par])
                        ln_prev_read = mk["ve_st_read"]
                        ln_sq_war[0] = ln_sq_war[1] = mk["pm_st"]
                        mk["_tcc"] = tcc
                        st["pend"].append(mk)
                        if tcc >= 1 and (flush_all or tcc <= 2):
                            ln2_flush(st)
                    if flush_all:
                        while st["pend"]:
                            ln2_flush(st)
                    return st

                def plan_router(st):
                    while st["pend"]:
                        ln2_flush(st)
                    ve_xn2 = ln_aff_prev
                    se_wa_prev = None
                    chain_tt = []
                    lgw0 = [ve_xn2]
                    for p_ in (0, 1):
                        if silu_done.get(p_):
                            lgw0.append(silu_done[p_])
                        if t3_done.get(p_):
                            lgw0.append(t3_done[p_])
                    bank_list = [h1a, h3a, h1b, h3b]
                    pm_lg_bank = []
                    for b_i, bank in enumerate(bank_list):
                        for sl in range(2):
                            tt = b_i * 2 + sl
                            tls = slice(tt * 128, (tt + 1) * 128)
                            for k in range(KD):
                                first = (sl == 0 and k == 0)
                                last = (sl == 1 and k == KD - 1)
                                P.op("tensor",
                                     lambda e, k=k, tls=tls, sl=sl, bank=bank,
                                     first=first, last=last: e.matmul(
                                         bank[:, 496 + sl * 8:504 + sl * 8],
                                         xn2T[:, k, tls], rw_bf[:, k, :],
                                         start=first, stop=last),
                                     waits=(lgw0 if (b_i == 0 and first) else []),
                                     inc=(pm, 1) if last else None)
                        pm_lg_bank.append(pm.mark())
                    lg_add_v = []
                    for tt in range(TQ // 128):
                        b_i, sl = divmod(tt, 2)
                        P.op("vector",
                             lambda e, tt=tt, sl=sl, bank=bank_list[tt // 2]:
                             e.tensor_add(
                                 lg_s[:, tt, :], bank[:, 496 + sl * 8:504 + sl * 8],
                                 biasB[:, :]),
                             waits=[pm_lg_bank[b_i]], inc=(ve, 1), chain=vch)
                        lg_add_v.append(ve.mark())
                    st["ve_lg_last"] = lg_add_v[-1]

                def plan_chains(st):
                    se_wa_prev = None
                    chain_tt = []
                    for tt in range(TQ // 128):
                        V = lambda fn, w=(), inc=None: P.op("vector", fn, waits=w, inc=inc, chain=vch)
                        V(lambda e, tt=tt: e.reduce_max(m1c[:, 0:1], lg_s[:, tt, :], axis=AX.X))
                        V(lambda e, tt=tt: e.tensor_scalar(tp_a[:, :], lg_s[:, tt, :], m1c[:, 0:1], None, ALU.is_ge))
                        V(lambda e: e.tensor_copy(tp_b[:, 0:1], tp_a[:, 0:1]))
                        V(lambda e: e.tensor_add(tp_b[:, 1:8], tp_a[:, 1:8], tp_a[:, 0:7]))
                        V(lambda e: e.tensor_copy(tp_c[:, 0:2], tp_b[:, 0:2]))
                        V(lambda e: e.tensor_add(tp_c[:, 2:8], tp_b[:, 2:8], tp_b[:, 0:6]))
                        V(lambda e: e.tensor_copy(tp_d[:, 0:4], tp_c[:, 0:4]))
                        V(lambda e: e.tensor_add(tp_d[:, 4:8], tp_c[:, 4:8], tp_c[:, 0:4]))
                        V(lambda e: e.tensor_sub(tp_d[:, :], tp_d[:, :], tp_a[:, :]))
                        V(lambda e: e.tensor_scalar(tp_d[:, :], tp_d[:, :], 0.5, None, ALU.is_lt))
                        V(lambda e: e.tensor_mul(tp_a[:, :], tp_a[:, :], tp_d[:, :]))
                        V(lambda e, tt=tt: e.scalar_tensor_tensor(
                            tp_b[:, :], tp_a[:, :], -1e30, lg_s[:, tt, :], ALU.mult, ALU.add))
                        V(lambda e: e.reduce_max(m1c[:, 1:2], tp_b[:, :], axis=AX.X))
                        V(lambda e: e.tensor_scalar(tp_c[:, :], tp_b[:, :], m1c[:, 1:2], None, ALU.is_ge))
                        V(lambda e: e.tensor_copy(tp_d[:, 0:1], tp_c[:, 0:1]))
                        V(lambda e: e.tensor_add(tp_d[:, 1:8], tp_c[:, 1:8], tp_c[:, 0:7]))
                        V(lambda e: e.tensor_copy(tp_b[:, 0:2], tp_d[:, 0:2]))
                        V(lambda e: e.tensor_add(tp_b[:, 2:8], tp_d[:, 2:8], tp_d[:, 0:6]))
                        V(lambda e: e.tensor_copy(tp_d[:, 0:4], tp_b[:, 0:4]))
                        V(lambda e: e.tensor_add(tp_d[:, 4:8], tp_b[:, 4:8], tp_b[:, 0:4]))
                        V(lambda e: e.tensor_sub(tp_d[:, :], tp_d[:, :], tp_c[:, :]))
                        V(lambda e: e.tensor_scalar(tp_d[:, :], tp_d[:, :], 0.5, None, ALU.is_lt))
                        V(lambda e: e.tensor_mul(tp_c[:, :], tp_c[:, :], tp_d[:, :]))
                        V(lambda e: e.tensor_sub(m1c[:, 2:3], m1c[:, 0:1], m1c[:, 1:2]),
                          w=([se_wa_prev] if se_wa_prev else []), inc=(ve, 1))
                        ve_d = ve.mark()
                        P.op("scalar", lambda e: e.activation(m1c[:, 3:4], m1c[:, 2:3],
                                                              AF.Sigmoid, scale=c_mone),
                             waits=[ve_d], inc=(se, 1), chain=sch)
                        se_wb = se.mark()
                        P.op("scalar", lambda e: e.activation(m1c[:, 2:3], m1c[:, 2:3],
                                                              AF.Sigmoid),
                             inc=(se, 1), chain=sch)
                        se_wa = se.mark()
                        se_wa_prev = se_wa
                        V(lambda e: e.tensor_scalar(
                            tp_a[:, :], tp_a[:, :], m1c[:, 2:3], None, ALU.mult),
                          w=[se_wa])
                        V(lambda e, tt=tt: e.scalar_tensor_tensor(
                            cw_tiles[:, tt, :], tp_c[:, :], m1c[:, 3:4], tp_a[:, :],
                            ALU.mult, ALU.add),
                          w=[se_wb] + ([ct_mark_prev_q] if ct_mark_prev_q else []),
                          inc=(ve, 1))
                        chain_tt.append(ve.mark())
                    st["chain_tt"] = chain_tt

                def plan_cts(st):
                    nonlocal ct_mark_prev_q, cwf_last
                    chain_tt = st["chain_tt"]
                    ctv = None
                    for tt in range(TQ // 128):
                        tls = slice(tt * 128, (tt + 1) * 128)
                        tw = [chain_tt[tt], (wsm, 16 * N_SMALL)]
                        if tt == 0:
                            tw.append(ln_prev_read)
                            if cwf_last:
                                tw.append(cwf_last)
                        if ctv:
                            tw.append(ctv)
                        P.op("tensor",
                             lambda e, tt=tt: e.transpose(
                                 ct_ps, cw_tiles[:, tt, :], idf[:, :]),
                             waits=tw, inc=(pm, 1))
                        pm_ct = pm.mark()
                        P.op("vector",
                             lambda e, tls=tls: e.tensor_copy(cw_rows[:, tls], ct_ps),
                             waits=[pm_ct], inc=(ve, 1))
                        ctv = ve.mark()
                    ct_mark_prev_q = pm.mark()
                    return ctv

                pro_st = plan_ln2(0, flush_all=True)
                plan_router(pro_st)
                plan_chains(pro_st)
                nxt_st = None
                for q in range(NQ):
                    aff_half = pro_st["aff_half"]
                    ve_lg_last = pro_st["ve_lg_last"]
                    pro_free = None
                    ct_planned = [False]
                    for tc in range(2):
                        tls = slice(tc * 512, (tc + 1) * 512)
                        for e_i in range(NE):
                            routed = e_i >= 2
                            er = e_i - 2
                            if routed:
                                fw = []
                                if prev_cwb_pm is not None:
                                    fw.append(prev_cwb_pm)
                                if pro_free is not None:
                                    fw.append(pro_free)
                                P.dma("sync",
                                      lambda e, er=er: e.dma_start(
                                          cw_flat[0:1, :], cw_rows[er:er + 1, :]),
                                      waits=fw, inc=(wsm, 16))
                                cwf_done = wsm.mark()
                                cwf_last = cwf_done
                            def plan_chunk_dma(c, e_i=e_i):
                                m = chunk_gctr[0]
                                cb = m % 2
                                wlp = wl0 if cb == 0 else wl1
                                cw_waits = [(wpe[e_i], 48)]
                                if m >= 2:
                                    cw_waits.append((wfr, m - 1))
                                P.dma("sync",
                                      lambda e, e_i=e_i, c=c, cb=cb: e.dma_start(
                                          w1c[:, cb, :, :],
                                          w1s[e_i, :, c * 512:(c + 1) * 512]
                                          .rearrange("(k p) h -> p k h", p=128)),
                                      waits=cw_waits, inc=(wlp, 16))
                                P.dma("sync",
                                      lambda e, e_i=e_i, c=c, cb=cb: e.dma_start(
                                          w3c[:, cb, :, :],
                                          w3s[e_i, :, c * 512:(c + 1) * 512]
                                          .rearrange("(k p) h -> p k h", p=128)),
                                      inc=(wlp, 16))
                                chunk_marks[m] = wlp.mark()
                                chunk_gctr[0] += 1

                            plan_chunk_dma(0)
                            plan_chunk_dma(1)
                            plan_chunk_dma(2)
                            plan_chunk_dma(3)
                            w2w = [(wpe[e_i], 48)]
                            if grp_seq > 0:
                                w2w.append((w2fr, grp_seq))
                            for j in range(4):
                                P.dma("sync",
                                      lambda e, e_i=e_i, j=j: e.dma_start(
                                          w2e[:, j * 4:(j + 1) * 4, :],
                                          w2s[e_i, j * 512:(j + 1) * 512, :]
                                          .rearrange("(t p) d -> p t d", p=128)),
                                      waits=w2w if j == 0 else [],
                                      inc=(wl2, 16))
                            wl2_e = wl2.mark()
                            grp_seq += 1

                            if routed:
                                cww = [cwf_done]
                                if silu_done.get(0):
                                    cww.append(silu_done[0])
                                if t3_done.get(0):
                                    cww.append(t3_done[0])
                                P.op("tensor",
                                     lambda e: e.matmul(
                                         h1a[:, :], ones_r_bf[:, :],
                                         cw_flat[0:1, 0:512],
                                         start=True, stop=True),
                                     waits=cww)
                                P.op("tensor",
                                     lambda e: e.matmul(
                                         h3a[:, :], ones_r_bf[:, :],
                                         cw_flat[0:1, 512:TQ],
                                         start=True, stop=True),
                                     inc=(pm, 1))
                                pm_cwb = pm.mark()
                                prev_cwb_pm = pm_cwb
                                P.op("vector",
                                     lambda e: e.tensor_copy(cwB[:, 0:512], h1a[:, :]),
                                     waits=[pm_cwb])
                                P.op("vector",
                                     lambda e: e.tensor_copy(cwB[:, 512:TQ], h3a[:, :]),
                                     inc=(ve, 1))
                                cwb_ev = ve.mark()
                                silu_done[0] = cwb_ev
                                t3_done[0] = cwb_ev

                            for c in range(4):
                                m = consume_gctr[0]
                                cb = m % 2
                                wl_c = chunk_marks[m]
                                for hcc in range(4):
                                    hc = 4 * c + hcc
                                    p_ = hc % 2
                                    hw = [wl_c]
                                    if tc == 0 and e_i == 0 and hc in (0, 1):
                                        if hc == 0:
                                            hw.append(aff_half[0])
                                        hw.append(ve_lg_last)
                                    if tc == 1 and e_i == 0 and hc == 0:
                                        hw.append(aff_half[1])
                                    if silu_done.get(p_):
                                        hw.append(silu_done[p_])
                                    if t3_done.get(p_):
                                        hw.append(t3_done[p_])
                                    for k in range(KD):
                                        P.op("tensor",
                                             lambda e, k=k, hcc=hcc, p_=p_, cb=cb, tls=tls: e.matmul(
                                                 h1[p_][:, :],
                                                 w1c[:, cb, k, hcc * 128:(hcc + 1) * 128],
                                                 xn2T[:, k, tls],
                                                 start=(k == 0), stop=(k == KD - 1)),
                                             waits=hw if k == 0 else [],
                                             inc=(p1, 1) if k == KD - 1 else None)
                                    p1_hc = p1.mark()
                                    lastchunk = (hcc == 3)
                                    for k in range(KD):
                                        incs = []
                                        if k == KD - 1:
                                            incs.append((p3, 1))
                                            if lastchunk:
                                                incs.append((wfr, 1))
                                        P.op("tensor",
                                             lambda e, k=k, hcc=hcc, p_=p_, cb=cb, tls=tls: e.matmul(
                                                 h3[p_][:, :],
                                                 w3c[:, cb, k, hcc * 128:(hcc + 1) * 128],
                                                 xn2T[:, k, tls],
                                                 start=(k == 0), stop=(k == KD - 1)),
                                             incs=incs)
                                    p3_hc = p3.mark()
                                    sw = [p1_hc]
                                    vw = [p3_hc]
                                    if g_prev.get(p_):
                                        sw.append(g_prev[p_])
                                        vw.append(g_prev[p_])
                                    P.op("scalar",
                                         lambda e, p_=p_: e.activation(
                                             s1b[:, p_, :], h1[p_][:, :], AF.Silu),
                                         waits=sw, inc=(se, 1))
                                    silu_done[p_] = se.mark()
                                    if routed:
                                        P.op("vector",
                                             lambda e, p_=p_, tls=tls: e.tensor_mul(
                                                 t3b[:, p_, :], h3[p_][:, :], cwB[:, tls]),
                                             waits=vw, inc=(ve, 1))
                                        t3_done[p_] = ve.mark()
                                    else:
                                        P.op("scalar",
                                             lambda e, p_=p_: e.copy(
                                                 t3b[:, p_, :], h3[p_][:, :]),
                                             waits=vw, inc=(se, 1))
                                        t3_done[p_] = se.mark()
                                    gw = [silu_done[p_], t3_done[p_]]
                                    if hc in gfree_marks:
                                        gw.append(gfree_marks[hc])
                                    P.op("gpsimd",
                                         lambda e, hc=hc, p_=p_: e.tensor_mul(
                                             gbuf[:, hc, :], s1b[:, p_, :], t3b[:, p_, :]),
                                         waits=gw, inc=(gg, 1))
                                    g_marks[hc] = gg.mark()
                                    g_prev[p_] = g_marks[hc]
                                consume_gctr[0] += 1

                            if tc == 0 and e_i == 1 and not ct_planned[0]:
                                pro_free = plan_cts(pro_st)
                                ct_planned[0] = True

                            col0 = tc * 512
                            for dt in range(KD):
                                bk = dt % 2
                                ow = [wl2_e]
                                if evac_bank_mark[bk] is not None:
                                    ow.append(evac_bank_mark[bk])
                                for hc in range(NH):
                                    incs = []
                                    if hc == NH - 1:
                                        incs.append((po, 1))
                                        if dt == KD - 1:
                                            incs.append((w2fr, 1))
                                    if dt == KD - 1:
                                        incs.append((gf, 1))
                                    w = []
                                    if dt == 0:
                                        w.append(g_marks[hc])
                                    if hc == 0:
                                        w.extend(ow)
                                    P.op("tensor",
                                         lambda e, hc=hc, dt=dt, bk=bk: e.matmul(
                                             o_ps[:, bk * 512:(bk + 1) * 512],
                                             w2e[:, hc, dt * 128:(dt + 1) * 128],
                                             gbuf[:, hc, :],
                                             start=(hc == 0), stop=(hc == NH - 1)),
                                         waits=w, incs=incs)
                                    if dt == KD - 1:
                                        gfree_marks[hc] = gf.mark()
                                po_m = po.mark()
                                if e_i == 0:
                                    P.op("vector",
                                         lambda e, dt=dt, bk=bk, col0=col0: e.tensor_copy(
                                             partial[:, dt, col0:col0 + 512],
                                             o_ps[:, bk * 512:(bk + 1) * 512]),
                                         waits=[po_m] + ([ep_sq_last] if
                                                         (tc == 0 and dt == 0 and ep_sq_last)
                                                         else []),
                                         inc=(ve, 1))
                                else:
                                    P.op("vector",
                                         lambda e, dt=dt, bk=bk, col0=col0: e.tensor_add(
                                             partial[:, dt, col0:col0 + 512],
                                             o_ps[:, bk * 512:(bk + 1) * 512],
                                             partial[:, dt, col0:col0 + 512]),
                                         waits=[po_m], inc=(ve, 1))
                                evac_bank_mark[bk] = ve.mark()
                            ot_evac = ve.mark()

                            if tc == 1 and e_i == 2 and q + 1 < NQ:
                                nxt_st = plan_ln2(q + 1)

                        xn2_war_half[tc] = p3.mark()

                        if tc == 1 and q + 1 < NQ:
                            while nxt_st["pend"]:
                                ln2_flush(nxt_st)
                        ep_mks = []
                        for tcc in (2 * tc, 2 * tc + 1):
                            ls = slice(tcc * 256, (tcc + 1) * 256)
                            par = ln_par[0] % 2
                            ln_par[0] += 1
                            epw = [ot_evac]
                            if pro_free is not None:
                                epw.append(pro_free)
                            mk = ln_stats_a(P, lambda k, ls=ls: partial[:, k, ls],
                                            stM, stQ, ln_prev_read,
                                            sq_war=ln_sq_war[par],
                                            extra_sq_waits=epw,
                                            rowset=ROWSF[par], sqbuf=SQF[par])
                            ln_prev_read = mk["ve_st_read"]
                            ln_sq_war[0] = ln_sq_war[1] = mk["pm_st"]
                            mk["_tcc"] = tcc
                            ep_mks.append(mk)
                        if tc == 1 and q + 1 < NQ:
                            plan_router(nxt_st)
                        for mk in ep_mks:
                            tcc = mk["_tcc"]
                            ls = slice(tcc * 256, (tcc + 1) * 256)
                            cs = slice(q * TQ + tcc * 256, q * TQ + (tcc + 1) * 256)
                            osp = os0 if out_seq % 2 == 0 else os1
                            ln_bcast(P, sB, msB, mk, prev_aff_done=ln_aff_prev)
                            gg_k = {}
                            for k in range(KD):
                                kp = k % 4
                                P.op("vector",
                                     lambda e, k=k, ls=ls: e.tensor_mul(
                                         t_aff[:, :], partial[:, k, ls], sB),
                                     waits=([mk["pm_bc"]] if k == 0 else [])
                                     + ([gg_k[k - 4]] if k >= 4 else []))
                                P.op("vector",
                                     lambda e, kp=kp: e.tensor_sub(
                                         t_af2[:, kp, :], t_aff[:, :], msB),
                                     inc=(ve, 1))
                                ve_t2 = ve.mark()
                                gw = [ve_t2]
                                if k == 0 and out_seq >= 1:
                                    gw.append(dma_marks[out_seq - 1])
                                P.op("gpsimd",
                                     lambda e, k=k, cs=cs, kp=kp: e.tensor_add(
                                         fin[:, k, :], X[:, k, cs], t_af2[:, kp, :]),
                                     waits=gw, inc=(gg, 1))
                                gg_k[k] = gg.mark()
                            mk["aff_done"] = ve.mark()
                            ln_aff_prev = mk["aff_done"]
                            gg_fin = gg.mark()
                            P.dma("gpsimd",
                                  lambda e, cs=cs: e.dma_start(
                                      outT_d.rearrange("(k p) t -> p k t", p=128)[:, :, cs],
                                      fin[:, :, :]),
                                  waits=[gg_fin], inc=(osp, 16))
                            dma_marks[out_seq] = osp.mark()
                            out_seq += 1
                        if tc == 1:
                            if q + 1 < NQ:
                                plan_chains(nxt_st)
                            ep_sq_last = ep_mks[-1].get("gg_sq")
                    if DEBUG and q == 0:
                        P.emit(no_gpsimd_drain=True)
                        tap("tap_cw", lambda: cw_rows[:, :])
                        tap("tap_xn2T", lambda: xn2T[:, :, :])
                        tap("tap_partial", lambda: partial[:, :, :])
                        P = Plan(nc, sems)
                    pro_st = nxt_st
                P.emit(no_gpsimd_drain=True)

        P = Plan(nc, sems)
        P.op("sync", lambda e: e.nop(), waits=[(os0, os0.n), (os1, os1.n)])
        P.emit(no_gpsimd_drain=True)

    return nc


_NC_CACHE = {}


def _get_nc():
    if "nc" not in _NC_CACHE:
        _NC_CACHE["nc"] = build_nc()
    return _NC_CACHE["nc"]


def kernel(L0_attn_hidden_states, L0_attn_queries, L0_attn_combiners,
           L0_ffn_router_w, L0_ffn_expert_bias, L0_ffn_w1, L0_ffn_w3,
           L0_ffn_w2, L0_ffn_sw1, L0_ffn_sw3, L0_ffn_sw2):
    hs = np.asarray(L0_attn_hidden_states, np.float32)
    queries = np.asarray(L0_attn_queries, np.float32)
    comb = np.asarray(L0_attn_combiners, np.float32)
    rw = np.asarray(L0_ffn_router_w, np.float32)
    bias = np.asarray(L0_ffn_expert_bias, np.float32)
    w1 = np.asarray(L0_ffn_w1, np.float32)
    w3 = np.asarray(L0_ffn_w3, np.float32)
    w2 = np.asarray(L0_ffn_w2, np.float32)
    sw1 = np.asarray(L0_ffn_sw1, np.float32)
    sw3 = np.asarray(L0_ffn_sw3, np.float32)
    sw2 = np.asarray(L0_ffn_sw2, np.float32)

    w1T = np.ascontiguousarray(
        np.concatenate([sw1, w1], axis=0).transpose(0, 2, 1))
    w3T = np.ascontiguousarray(
        np.concatenate([sw3, w3], axis=0).transpose(0, 2, 1))
    w2T = np.ascontiguousarray(
        np.concatenate([sw2, w2], axis=0).transpose(0, 2, 1))
    qwD = np.ascontiguousarray(
        queries.reshape(NA, HD, NA, 2, 128).transpose(4, 0, 2, 3, 1)
        .reshape(128, 32, HD))
    combD = np.ascontiguousarray(
        comb.reshape(NA, NA, 2, 128, HD).transpose(3, 0, 1, 2, 4)
        .reshape(128, 32, HD))
    rwD = np.ascontiguousarray(
        rw.T.reshape(KD, 128, 8).transpose(1, 0, 2))
    biasR = np.ascontiguousarray(bias.reshape(1, 8))

    nc = _get_nc()
    in_maps = []
    for b in range(NCORES):
        in_maps.append({
            "xT": np.ascontiguousarray(hs[b].T),
            "w1T": w1T, "w3T": w3T, "w2T": w2T,
            "qwD": qwD, "combD": combD, "rwD": rwD, "bias": biasR,
        })
    res = run_bass_kernel_spmd(nc, in_maps, core_ids=list(range(NCORES)))
    out = np.stack([res.results[b]["outT"].T for b in range(NCORES)], axis=0)
    return np.ascontiguousarray(out.astype(np.float32))
